# version 06
# baseline (speedup 1.0000x reference)
import numpy as np
import ml_dtypes
import contextlib
import concourse.bass as bass
import concourse.mybir as mybir
from concourse.bass_utils import run_bass_kernel_spmd

F32 = mybir.dt.float32
BF16 = mybir.dt.bfloat16
AF = mybir.ActivationFunctionType
ALU = mybir.AluOpType
AX = mybir.AxisListType

NCORE = 8
B, S, H = 16, 2048, 2048
NH, HD = 16, 128
FF = 8192
SPC = B // NCORE
TOK = SPC * S
TP = 512
NPASS = TOK // TP
import os as _os
if _os.environ.get("KNPASS"):
    NPASS = int(_os.environ["KNPASS"])
QG = 512
NG = S // QG
NI = H // 128
NHC = FF // 128
NFO = H // 128
EPS = 1e-5

ENGINES = ("sync", "gpsimd", "tensor", "vector", "scalar")


class Sem:
    def __init__(self, nc, stack, name):
        self.h = stack.enter_context(nc.semaphore(name))
        self.name = name
        self.count = 0


class Prog:

    def __init__(self, nc, auto=None):
        self.nc = nc
        self.ops = {e: [] for e in ENGINES}
        self.wm = {e: {} for e in ENGINES}
        self.auto = auto or {}

    def op(self, engine, fn, waits=(), incs=(), inc_each=False):
        eff = []
        for s, v in waits:
            if v is None or v <= 0:
                continue
            assert v <= s.count, f"wait on future of {s.name}: {v} > {s.count}"
            if self.wm[engine].get(s.name, 0) < v:
                self.wm[engine][s.name] = v
                eff.append((s, v))
        incs = list(incs)
        if engine in self.auto and not any(s is self.auto[engine] for s, _ in incs):
            incs.append((self.auto[engine], 1))
        swv = None
        for s, k in incs:
            s.count += 2 * k if engine == "tensor" else k
        if engine == "tensor" and incs:
            s0, k0 = incs[0]
            swv = (s0, s0.count - k0)
        self.ops[engine].append((eff, fn, incs, inc_each, swv))

    def emit(self):
        with self.nc.Block() as blk:
            for ename in ENGINES:
                ops = self.ops[ename]
                if not ops:
                    continue

                def make(ops, ename=ename):
                    def run(eng):
                        for waits, fn, incs, inc_each, swv in ops:
                            for s, v in waits:
                                eng.wait_ge(s.h, v)
                            ret = fn(eng)
                            if incs:
                                insts = ret if isinstance(ret, list) else [ret]
                                if ename == "tensor":
                                    last = insts[-1]
                                    for s, k in incs:
                                        last.then_inc(s.h, k)
                                    eng.wait_ge(swv[0].h, swv[1])
                                    nop = eng.nop()
                                    for s, k in incs:
                                        nop.then_inc(s.h, k)
                                else:
                                    tgt = insts if inc_each else insts[-1:]
                                    for inst in tgt:
                                        for s, k in incs:
                                            inst.then_inc(s.h, k)
                    return run

                getattr(blk, ename)(make(ops))


def _odd_perm():
    return list(range(32, 64)) + list(range(0, 32)) + list(range(64, 128))


def _host_prep_wqkv(wqkv):
    cols = []
    op = _odd_perm()
    for h in range(NH):
        base = h * 3 * HD
        q = list(range(base, base + 128))
        k = list(range(base + 128, base + 256))
        if h % 2 == 1:
            q = [q[i] for i in op]
            k = [k[i] for i in op]
        rotq = [base + ((m + 16) % 32) for m in range(32)]
        rotk = [base + 128 + ((m + 16) % 32) for m in range(32)]
        v = list(range(base + 256, base + 384))
        cols += q + k + rotq + rotk + v
    return np.ascontiguousarray(wqkv[:, cols])


def _host_tables():
    j = np.arange(16, dtype=np.float64)
    invf = 10000.0 ** (-j / 16.0)
    t = np.arange(S, dtype=np.float64)
    ang = t[None, :] * invf[:, None]
    cos16, sin16 = np.cos(ang), np.sin(ang)
    cos32 = np.concatenate([cos16, cos16], 0)
    sin32 = np.concatenate([-sin16, sin16], 0)
    cosp = np.concatenate([cos32, cos32], 0).astype(np.float32)
    sinp = np.concatenate([sin32, sin32], 0).astype(np.float32)
    idbf = np.eye(128, dtype=ml_dtypes.bfloat16)
    idf32 = np.eye(128, dtype=np.float32)
    mask = np.where(np.tril(np.ones((128, 128), bool)), 0.0, -30000.0).astype(np.float32)
    return cosp, sinp, idbf, idf32, mask


def build_program(debug=False, phases=("prep", "f1", "a0", "a1", "f3")):
    nc = bass.Bass("TRN2", target_bir_lowering=False)
    st = contextlib.ExitStack()

    x_in = nc.dram_tensor("x", [TOK, H], F32, kind="ExternalInput").ap()
    wg_t = nc.dram_tensor("wg_t", [H, FF], F32, kind="ExternalInput").ap()
    wu_t = nc.dram_tensor("wu_t", [H, FF], F32, kind="ExternalInput").ap()
    wd_t = nc.dram_tensor("wd_t", [FF, H], F32, kind="ExternalInput").ap()
    wqkv = nc.dram_tensor("wqkvp", [H, NH * 448], F32, kind="ExternalInput").ap()
    wo = nc.dram_tensor("wo", [H, H], F32, kind="ExternalInput").ap()
    t_cos = nc.dram_tensor("t_cos", [64, S], F32, kind="ExternalInput").ap()
    t_sin = nc.dram_tensor("t_sin", [64, S], F32, kind="ExternalInput").ap()
    t_idb = nc.dram_tensor("t_idb", [128, 128], BF16, kind="ExternalInput").ap()
    t_idf = nc.dram_tensor("t_idf", [128, 128], F32, kind="ExternalInput").ap()
    t_mask = nc.dram_tensor("t_mask", [128, 128], F32, kind="ExternalInput").ap()
    t_bq = nc.dram_tensor("t_bq", [128, NH], F32, kind="ExternalInput").ap()
    t_bk = nc.dram_tensor("t_bk", [128, NH], F32, kind="ExternalInput").ap()
    t_brq = nc.dram_tensor("t_brq", [64, NH // 2], F32, kind="ExternalInput").ap()
    t_brk = nc.dram_tensor("t_brk", [64, NH // 2], F32, kind="ExternalInput").ap()
    t_bvb = nc.dram_tensor("t_bvb", [128, H], F32, kind="ExternalInput").ap()
    t_bob = nc.dram_tensor("t_bob", [128, H], F32, kind="ExternalInput").ap()
    out = nc.dram_tensor("out", [TOK, H], F32, kind="ExternalOutput").ap()

    kd = "ExternalOutput" if debug else "Internal"
    x1s = nc.dram_tensor("x1s", [TOK, H], F32, kind=kd).ap()
    x2s = nc.dram_tensor("x2s", [TOK, H], F32, kind=kd).ap()
    s_wg = nc.dram_tensor("s_wg", [NHC, 128, 2048], BF16, kind="Internal").ap()
    s_wu = nc.dram_tensor("s_wu", [NHC, 128, 2048], BF16, kind="Internal").ap()
    s_wd = nc.dram_tensor("s_wd", [NFO, 128, FF], BF16, kind="Internal").ap()
    s_qk = nc.dram_tensor("s_qk", [NH, 128, NI * 256], BF16, kind="Internal").ap()
    s_rot = nc.dram_tensor("s_rot", [NH // 2, 128, NI * 128], BF16, kind="Internal").ap()
    s_wv = nc.dram_tensor("s_wv", [4, 128, NI * 512], BF16, kind="Internal").ap()
    s_wo = nc.dram_tensor("s_wo", [4, 128, NH * 512], BF16, kind="Internal").ap()
    k_scr = nc.dram_tensor("k_scr", [SPC, NH, 128, S], BF16, kind="Internal").ap()
    v_scr = nc.dram_tensor("v_scr", [SPC, NH, S // 128, 128, 128], BF16, kind="Internal").ap()

    sems = {}

    def sem(name):
        if name not in sems:
            sems[name] = Sem(nc, st, name)
        return sems[name]

    sPE, sDV, sAC = sem("sPE"), sem("sDV"), sem("sAC")
    auto_map = {"vector": sDV, "scalar": sAC}

    pp_h = st.enter_context(nc.psum_tensor("pp", [128, 4096], F32))
    pp = pp_h[:]
    pp_bf = pp_h.bitcast(BF16)[:]

    def bank_f(i, n=1):
        return pp[:, i * 512:(i + n) * 512]

    def bank_b(i, n=1):
        return pp_bf[:, i * 1024:(i + n) * 1024]

    cst_h = st.enter_context(nc.sbuf_tensor("cstcol", [128, 2], F32))
    epsb = cst_h[:, 0:1]
    nhalfb = cst_h[:, 1:2]

    def emit_ln(P, stt_, xg, xnm_dst, w_in, war_dv):
        sDV_, sAC_ = sems["sDV"], sems["sAC"]
        ssum, ssq, mvec, msq, var = (stt_[:, i:i + 1] for i in range(5))
        lnv, rstd = stt_[:, 5:6], stt_[:, 6:7]
        w0 = list(w_in)
        if war_dv:
            w0.append((sDV_, war_dv))
        P.op("vector", lambda eng: eng.reduce_sum(ssum, xg, axis=AX.X), waits=w0)
        P.op("scalar",
             lambda eng: eng.activation(xnm_dst, xg, AF.Square, accum_out=ssq),
             waits=w0)
        P.op("vector", lambda eng: eng.tensor_scalar_mul(mvec, ssum, 1.0 / H),
             waits=[(sDV_, sDV_.count)])
        P.op("vector", lambda eng: eng.tensor_mul(msq, mvec, mvec),
             waits=[(sDV_, sDV_.count)])
        P.op("vector", lambda eng: eng.scalar_tensor_tensor(
            var, ssq, 1.0 / H, msq, ALU.mult, ALU.subtract),
             waits=[(sAC_, sAC_.count), (sDV_, sDV_.count)])
        P.op("scalar", lambda eng: eng.activation(lnv, var, AF.Ln, bias=epsb),
             waits=[(sDV_, sDV_.count)])
        P.op("scalar", lambda eng: eng.activation(rstd, lnv, AF.Exp, scale=nhalfb),
             waits=[(sAC_, sAC_.count)])
        P.op("vector",
             (lambda: lambda eng: eng.tensor_scalar(
                 xnm_dst, xg, mvec, rstd, ALU.subtract, ALU.mult))(),
             waits=[(sAC_, sAC_.count)])
        return sDV_.count

    def phase_prologue(P):
        w = [(sPE, sPE.count), (sDV, sDV.count), (sAC, sAC.count)]
        if "prep" in sems:
            w.append((sems["prep"], sems["prep"].count))
        import os as _os5
        dly = int(_os5.environ.get("KDELAY", 0))
        for e in ENGINES:
            if dly:
                P.op(e, (lambda dly=dly: lambda eng: [eng.nop(cycle_cnt=dly) for _ in range(4)])(), waits=list(w))
            else:
                P.op(e, lambda eng: eng.nop(), waits=list(w))

    prep_ready = {}

    def emit_prep(P):
        sPR = sem("prep")
        jobs = []
        for c in range(NHC):
            for wk, srcw, dstw in (("wg", wg_t, s_wg), ("wu", wu_t, s_wu)):
                src = bass.AP(srcw.tensor, c * 128,
                              [[FF, 128], [128 * FF, NI], [1, 128]])
                jobs.append(((wk, c), src, dstw[c]))
        for f in range(NFO):
            src = bass.AP(wd_t.tensor, f * 128,
                          [[H, 128], [128 * H, NHC], [1, 128]])
            jobs.append((("wd", f), src, s_wd[f]))
        for h in range(NH):
            src = bass.AP(wqkv.tensor, h * 448,
                          [[NH * 448, 128], [128 * NH * 448, NI], [1, 256]])
            jobs.append((("qk", h), src, s_qk[h]))
        for jp in range(NH // 2):
            for qk in range(2):
                for e in range(2):
                    src = bass.AP(wqkv.tensor, (2 * jp + e) * 448 + 256 + qk * 32,
                                  [[NH * 448, 128], [128 * NH * 448, NI], [1, 32]])
                    dst = bass.AP(s_rot.tensor, jp * 128 * NI * 128 + qk * 64 + e * 32,
                                  [[NI * 128, 128], [128, NI], [1, 32]])
                    jobs.append((("rot", jp, qk, e), src, dst))
        for s4 in range(4):
            for u in range(4):
                src = bass.AP(wqkv.tensor, (4 * s4 + u) * 448 + 320,
                              [[NH * 448, 128], [128 * NH * 448, NI], [1, 128]])
                dst = bass.AP(s_wv.tensor, s4 * 128 * NI * 512 + u * 128,
                              [[NI * 512, 128], [512, NI], [1, 128]])
                jobs.append((("wv", s4, u), src, dst))
        for g in range(4):
            src = bass.AP(wo.tensor, g * 512,
                          [[H, 128], [128 * H, NH], [1, 512]])
            jobs.append((("wo", g), src, s_wo[g]))

        P.op("vector", lambda eng: eng.memset(epsb, EPS))
        P.op("vector", lambda eng: eng.memset(nhalfb, -0.5), incs=[(sems["sDV"], 1)])
        batch2 = [j for j in jobs if j[0][0] in ("qk", "rot", "wv", "wo")]
        batch1 = [j for j in jobs if j[0][0] not in ("qk", "rot", "wv", "wo")]
        gu = [j for j in batch1 if j[0][0] != "wd"]
        wd = [j for j in batch1 if j[0][0] == "wd"]
        batch1 = []
        for i in range(NHC):
            batch1 += gu[2 * i:2 * i + 2]
            if i < NFO:
                batch1.append(wd[i])
        for key, src, dst in batch1:
            P.op("gpsimd",
                 (lambda src=src, dst=dst: lambda eng: eng.dma_start(dst, src))(),
                 waits=[(sPR, sPR.count)], incs=[(sPR, 16)])
            prep_ready[key] = sPR.count
        prep_ready["_batch2"] = batch2
        return sPR

    def emit_ffn(P, st, src, dst, tag, sPR, prep_ready):
        import os as _os
        _stages = _os.environ.get("KSTAGE", "ABC")
        xin = [st.enter_context(nc.sbuf_tensor(f"{tag}_xin{i}", [128, H], F32)) for i in range(2)]
        xnm = [st.enter_context(nc.sbuf_tensor(f"{tag}_xnm{i}", [128, H], BF16)) for i in range(2)]
        xnT = st.enter_context(nc.sbuf_tensor(f"{tag}_xnT", [128, NI * TP], BF16))
        hbuf = st.enter_context(nc.sbuf_tensor(f"{tag}_h", [128, NHC * TP], BF16))
        wgb = [st.enter_context(nc.sbuf_tensor(f"{tag}_wg{i}", [128, 2048], BF16)) for i in range(2)]
        wub = [st.enter_context(nc.sbuf_tensor(f"{tag}_wu{i}", [128, 2048], BF16)) for i in range(2)]
        wdb = [st.enter_context(nc.sbuf_tensor(f"{tag}_wd{i}", [128, 4096], BF16)) for i in range(4)]
        slu = [st.enter_context(nc.sbuf_tensor(f"{tag}_slu{i}", [128, 512], F32)) for i in range(2)]
        fmo = [st.enter_context(nc.sbuf_tensor(f"{tag}_fmo{i}", [128, 512], F32)) for i in range(2)]
        xsl = [st.enter_context(nc.sbuf_tensor(f"{tag}_xsl{i}", [128, 512], F32)) for i in range(2)]
        pcb = [st.enter_context(nc.sbuf_tensor(f"{tag}_pcb{i}", [128, 512], F32)) for i in range(2)]
        idb = st.enter_context(nc.sbuf_tensor(f"{tag}_idb", [128, 128], BF16))
        idf = st.enter_context(nc.sbuf_tensor(f"{tag}_idf", [128, 128], F32))
        stt_ = st.enter_context(nc.sbuf_tensor(f"{tag}_st", [128, 32], F32))

        sXIN = [sem(f"{tag}xin{i}") for i in range(2)]
        sWG = [sem(f"{tag}wg{i}") for i in range(2)]
        sWU = [sem(f"{tag}wu{i}") for i in range(2)]
        sWD = [sem(f"{tag}wd{i}") for i in range(4)]
        sXSL = [sem(f"{tag}xsl{i}") for i in range(2)]
        sPC = [sem(f"{tag}pc{i}") for i in range(2)]
        sID = sem(f"{tag}id")

        P.op("sync", lambda eng: eng.dma_start(idb[:], t_idb), incs=[(sID, 16)])
        P.op("sync", lambda eng: eng.dma_start(idf[:], t_idf), incs=[(sID, 16)])
        id_wm = sID.count

        evacA, gate_pe, up_pe, mult_dv = {}, {}, {}, {}
        down_pe, stt_dv, fmo_dv, xnm_dv = {}, {}, {}, {}

        def prev2(d, p, i, n):
            if i >= 2:
                return d.get((p, i - 2))
            if p >= 1:
                return d.get((p - 1, i + n - 2))
            return None

        xnTr = {}

        def emitA(p):
            row0 = p * TP
            for s in range(4):
                sl = s % 2
                waits = [(sID, id_wm)]
                pv = prev2(xnm_dv, p, s, 4)
                if pv:
                    waits.append((sDV, pv))
                P.op("sync",
                     (lambda s=s, sl=sl, row0=row0:
                      lambda eng: eng.dma_start(xin[sl][:], src[row0 + s * 128:row0 + (s + 1) * 128, :]))(),
                     waits=waits, incs=[(sXIN[sl], 16)])
                w_in = [(sXIN[sl], sXIN[sl].count)]
                war = xnm_dv.get((p, s - 1)) or (xnm_dv.get((p - 1, 3)) if p else None)
                xnm_dv[(p, s)] = emit_ln(P, stt_, xin[sl][:], xnm[sl][:], w_in, war)
                pk = 4 + 2 * sl
                waits = [(sDV, sDV.count)]
                pv = prev2(evacA, p, s, 4)
                if pv:
                    waits.append((sAC, pv))
                def fnT(sl=sl, pk=pk):
                    def go(eng):
                        return [eng.transpose(bank_b(pk, 2)[:, I * 128:(I + 1) * 128],
                                              xnm[sl][:, I * 128:(I + 1) * 128], idb[:])
                                for I in range(NI)]
                    return go
                P.op("tensor", fnT(), waits=waits, incs=[(sPE, 1)])
                def fnE(pk=pk, s=s):
                    def go(eng):
                        dstp = bass.AP(xnT, s * 128, [[NI * TP, 128], [TP, NI], [1, 128]])
                        return eng.copy(dstp, bank_b(pk, 2))
                    return go
                P.op("scalar", fnE(), waits=[(sPE, sPE.count)], incs=[(sAC, 1)])
                evacA[(p, s)] = sAC.count
            xnTr[p] = sAC.count

        emitA(0)
        for p in range(NPASS):
            row0 = p * TP
            xnT_ready = xnTr[p]

            for c in range(NHC if "B" in _stages else 0):
                sl = c % 2
                for wk, ring, buf, scr, prevk in (("wg", sWG, wgb, s_wg, gate_pe),
                                                  ("wu", sWU, wub, s_wu, up_pe)):
                    waits = []
                    if sPR is not None:
                        waits.append((sPR, prep_ready[(wk, c)]))
                    pv = prev2(prevk, p, c, NHC)
                    if pv:
                        waits.append((sPE, pv))
                    P.op("sync",
                         (lambda buf=buf, scr=scr, sl=sl, c=c:
                          lambda eng: eng.dma_start(buf[sl][:], scr[c]))(),
                         waits=waits, incs=[(ring[sl], 16)])
                for which, (ring, buf, bk, prevk) in enumerate(
                        ((sWG, wgb, 2 * sl, gate_pe), (sWU, wub, 2 * sl + 1, up_pe))):
                    waits = [(ring[sl], ring[sl].count), (sAC, xnT_ready)]
                    pv = prev2(mult_dv, p, c, NHC)
                    if pv:
                        waits.append((sDV, pv))
                    if p and c < 2 and which == 0:
                        waits.append((sDV, stt_dv[(p - 1, NFO - 1)]))
                    def fnMM(buf=buf, sl=sl, bk=bk):
                        def go(eng):
                            return [eng.matmul(bank_f(bk), buf[sl][:, i * 128:(i + 1) * 128],
                                               xnT[:, i * TP:(i + 1) * TP],
                                               start=(i == 0), stop=(i == NI - 1))
                                    for i in range(NI)]
                        return go
                    P.op("tensor", fnMM(), waits=waits, incs=[(sPE, 1)])
                    prevk[(p, c)] = sPE.count
                waits = [(sPE, gate_pe[(p, c)])]
                pv = prev2(mult_dv, p, c, NHC)
                if pv:
                    waits.append((sDV, pv))
                P.op("scalar",
                     (lambda sl=sl: lambda eng: eng.activation(slu[sl][:], bank_f(2 * sl), AF.Silu))(),
                     waits=waits, incs=[(sAC, 1)])
                P.op("vector",
                     (lambda sl=sl, c=c: lambda eng: eng.tensor_mul(
                         hbuf[:, c * TP:(c + 1) * TP], slu[sl][:], bank_f(2 * sl + 1)))(),
                     waits=[(sAC, sAC.count), (sPE, up_pe[(p, c)])], incs=[(sDV, 1)])
                mult_dv[(p, c)] = sDV.count
            h_ready = sDV.count

            for f in range((NFO + 1) if "C" in _stages else 0):
                if f < NFO:
                    for hf in range(2):
                        hidx = 2 * f + hf
                        slw = hidx % 4
                        waits = []
                        if sPR is not None:
                            waits.append((sPR, prep_ready[("wd", f)]))
                        if hidx >= 4:
                            pvf = f - 2
                            pv = down_pe.get((p, pvf)) if pvf >= 0 else \
                                (down_pe.get((p - 1, pvf + NFO)) if p else None)
                            if pv:
                                waits.append((sPE, pv))
                        P.op("sync",
                             (lambda f=f, hf=hf, slw=slw:
                              lambda eng: eng.dma_start(
                                  wdb[slw][:],
                                  bass.AP(s_wd.tensor, f * 128 * FF + hf * 4096,
                                          [[FF, 128], [1, 4096]])))(),
                             waits=waits, incs=[(sWD[slw], 16)])
                    sl = f % 2
                    waits = []
                    pv = prev2(stt_dv, p, f, NFO)
                    if pv:
                        waits.append((sDV, pv))
                    P.op("sync",
                         (lambda f=f, sl=sl, row0=row0:
                          lambda eng: eng.dma_start(
                              xsl[sl][:].rearrange("p (s m) -> p s m", s=4),
                              bass.AP(src.tensor, row0 * H + f * 128,
                                      [[H, 128], [128 * H, 4], [1, 128]])))(),
                         waits=waits, incs=[(sXSL[sl], 16)])
                    for hf in range(2):
                        slw = (2 * f + hf) % 4
                        waits = [(sWD[slw], sWD[slw].count), (sDV, h_ready)]
                        if hf == 0:
                            pv = prev2(fmo_dv, p, f, NFO)
                            if pv:
                                waits.append((sDV, pv))
                            if p == 0 and f < 2:
                                pass
                            if f < 2:
                                pass
                        def fnD(sl=sl, slw=slw, hf=hf):
                            def go(eng):
                                return [eng.matmul(bank_f(sl), wdb[slw][:, cc * 128:(cc + 1) * 128],
                                                   hbuf[:, (hf * 32 + cc) * TP:(hf * 32 + cc + 1) * TP],
                                                   start=(hf == 0 and cc == 0),
                                                   stop=(hf == 1 and cc == 31))
                                        for cc in range(32)]
                            return go
                        P.op("tensor", fnD(), waits=waits, incs=[(sPE, 1)])
                    down_pe[(p, f)] = sPE.count
                    P.op("vector",
                         (lambda sl=sl: lambda eng: eng.tensor_copy(fmo[sl][:], bank_f(sl)))(),
                         waits=[(sPE, sPE.count)], incs=[(sDV, 1)])
                    fmo_dv[(p, f)] = sDV.count
                fm1 = f - 1
                if fm1 >= 0:
                    sl1 = fm1 % 2
                    pk = 2 + sl1
                    waits = [(sDV, fmo_dv[(p, fm1)])]
                    pv = prev2(stt_dv, p, fm1, NFO)
                    if pv:
                        waits.append((sDV, pv))
                    if p == 0 and fm1 < 2:
                        waits.append((sDV, h_ready))
                    def fnT2(sl1=sl1, pk=pk):
                        def go(eng):
                            return [eng.transpose(bank_f(pk)[:, jj * 128:(jj + 1) * 128],
                                                  fmo[sl1][:, jj * 128:(jj + 1) * 128], idf[:])
                                    for jj in range(4)]
                        return go
                    P.op("tensor", fnT2(), waits=waits, incs=[(sPE, 1)])
                    waits = [(sPE, sPE.count), (sXSL[sl1], sXSL[sl1].count),
                             (sPC[sl1], sPC[sl1].count)]
                    def fnS(sl1=sl1, pk=pk):
                        def go(eng):
                            return eng.scalar_tensor_tensor(pcb[sl1][:], bank_f(pk), 0.5,
                                                            xsl[sl1][:], ALU.mult, ALU.add)
                        return go
                    P.op("vector", fnS(), waits=waits, incs=[(sDV, 1)])
                    stt_dv[(p, fm1)] = sDV.count
                    P.op("gpsimd",
                         (lambda sl1=sl1, fm1=fm1, row0=row0:
                          lambda eng: eng.dma_start(
                              bass.AP(dst.tensor, row0 * H + fm1 * 128,
                                      [[H, 128], [128 * H, 4], [1, 128]]),
                              pcb[sl1][:].rearrange("p (s m) -> p s m", s=4)))(),
                         waits=[(sDV, sDV.count)], incs=[(sPC[sl1], 16)])
                if f == 8 and p + 1 < NPASS:
                    emitA(p + 1)
        P.op("gpsimd", lambda eng: eng.nop(),
             waits=[(sPC[0], sPC[0].count), (sPC[1], sPC[1].count)])
        drains = [(s, s.count) for s in
                  [sID] + sXIN + sWG + sWU + sWD + sXSL]
        P.op("sync", lambda eng: eng.nop(), waits=drains)

    def emit_attn(P, st, seq, tag):
        x1g = st.enter_context(nc.sbuf_tensor(f"{tag}_x1g", [128, 4 * H], F32))
        xnm = [st.enter_context(nc.sbuf_tensor(f"{tag}_xnm{i}", [128, H], BF16)) for i in range(2)]
        xnT = st.enter_context(nc.sbuf_tensor(f"{tag}_xnT", [128, NI * QG], BF16))
        wqk = [st.enter_context(nc.sbuf_tensor(f"{tag}_wqk{i}", [128, NI * 256], BF16)) for i in range(2)]
        wrtb = st.enter_context(nc.sbuf_tensor(f"{tag}_wrt", [128, NI * 128], BF16))
        wvb = st.enter_context(nc.sbuf_tensor(f"{tag}_wv", [128, NI * 512], BF16))
        v4 = [st.enter_context(nc.sbuf_tensor(f"{tag}_v4{i}", [128, 4 * 512], BF16)) for i in range(2)]
        kwk = [st.enter_context(nc.sbuf_tensor(f"{tag}_kwk{i}", [128, S], BF16)) for i in range(2)]
        vwk = [st.enter_context(nc.sbuf_tensor(f"{tag}_vwk{i}", [128, S], BF16)) for i in range(2)]
        qwk = [st.enter_context(nc.sbuf_tensor(f"{tag}_qwk{i}", [128, QG], BF16)) for i in range(2)]
        rotbuf = [st.enter_context(nc.sbuf_tensor(f"{tag}_rob{i}", [64, 1024], F32))
                  for i in range(2)]
        psb = [st.enter_context(nc.sbuf_tensor(f"{tag}_p{i}", [128, S], BF16)) for i in range(2)]
        ptb = st.enter_context(nc.sbuf_tensor(f"{tag}_pt", [128, 16 * 512], BF16))
        wob = ptb
        afm = st.enter_context(nc.sbuf_tensor(f"{tag}_afm", [128, NH * 512], BF16))
        tmp1 = st.enter_context(nc.sbuf_tensor(f"{tag}_tmp1", [64, 512], F32))
        tmp2 = st.enter_context(nc.sbuf_tensor(f"{tag}_tmp2", [64, 512], F32))
        otmp = st.enter_context(nc.sbuf_tensor(f"{tag}_otmp", [128, 512], F32))
        idb = st.enter_context(nc.sbuf_tensor(f"{tag}_idb", [128, 128], BF16))
        cosb = st.enter_context(nc.sbuf_tensor(f"{tag}_cos", [64, S], F32))
        sinb = st.enter_context(nc.sbuf_tensor(f"{tag}_sin", [64, S], F32))
        maskb = st.enter_context(nc.sbuf_tensor(f"{tag}_msk", [128, 128], F32))
        bqb = st.enter_context(nc.sbuf_tensor(f"{tag}_bq", [128, NH], F32))
        bkb = st.enter_context(nc.sbuf_tensor(f"{tag}_bk", [128, NH], F32))
        brqb = st.enter_context(nc.sbuf_tensor(f"{tag}_brq", [64, NH // 2], F32))
        brkb = st.enter_context(nc.sbuf_tensor(f"{tag}_brk", [64, NH // 2], F32))
        bvbb = st.enter_context(nc.sbuf_tensor(f"{tag}_bvb", [128, H], F32))
        bobb = st.enter_context(nc.sbuf_tensor(f"{tag}_bob", [128, H], F32))
        stt_ = st.enter_context(nc.sbuf_tensor(f"{tag}_st", [128, 32], F32))
        negmax = stt_[:, 28:29]
        den = st.enter_context(nc.sbuf_tensor(f"{tag}_den", [128, 2], F32))

        sCst = sem(f"{tag}cst")
        sX1 = [sem(f"{tag}x1_{i}") for i in range(4)]
        sWQK = [sem(f"{tag}wqk{i}") for i in range(2)]
        sWRT = sem(f"{tag}wrt")
        sWV = sem(f"{tag}wv")
        sWO = [sem(f"{tag}wo0")]
        sKH = [sem(f"{tag}kh{i}") for i in range(2)]
        sVH = [sem(f"{tag}vh{i}") for i in range(2)]
        sKW = sem(f"{tag}kw")
        sVW = sem(f"{tag}vw")
        sX2 = sem(f"{tag}x2")

        for dst_, src_ in ((idb, t_idb), (cosb, t_cos), (sinb, t_sin), (maskb, t_mask),
                           (bqb, t_bq), (bkb, t_bk), (brqb, t_brq), (brkb, t_brk),
                           (bvbb, t_bvb), (bobb, t_bob)):
            P.op("sync",
                 (lambda dst_=dst_, src_=src_: lambda eng: eng.dma_start(dst_[:], src_))(),
                 incs=[(sCst, 16)])
        cst_wm = sCst.count

        evacA, qk_pe, rot_pe, rot_evac, rope_dv, rope_ac_d = {}, {}, {}, {}, {}, {}
        vprojE, vev_d, vcur_d, kw_d = {}, {}, {}, {}
        scores_pe, exp_ac, tpos_pe, pscale_d = {}, {}, {}, {}
        pt_ac, pv_pe, attn_dv = {}, {}, {}
        op_pe, op_dv, xnm_dv = {}, {}, {}
        wqk_ld, wrt_ld, wv_ld, hist_kh, hist_vh = {}, {}, {}, {}, {}
        last_vev = [0]
        ptev_last = [0]

        for g in range(NG):
            grow = seq * S + g * QG
            pos0 = g * QG
            for s_ in range(4):
                sl = s_ % 2
                waits = [(sCst, cst_wm)]
                if g > 0 and s_ == 0:
                    waits.append((sX2, sX2.count))
                P.op("sync",
                     (lambda s_=s_, grow=grow:
                      lambda eng: eng.dma_start(x1g[:, s_ * H:(s_ + 1) * H],
                                                x1s[grow + s_ * 128:grow + (s_ + 1) * 128, :]))(),
                     waits=waits, incs=[(sX1[s_], 16)])
                w_in = [(sX1[s_], sX1[s_].count)]
                xg = x1g[:, s_ * H:(s_ + 1) * H]
                war = xnm_dv.get((g, s_ - 1)) or (xnm_dv.get((g - 1, 3)) if g else None)
                xnm_dv[(g, s_)] = emit_ln(P, stt_, xg, xnm[sl][:], w_in, war)
                pk = 4 + 2 * sl
                waits = [(sDV, sDV.count)]
                pv = evacA.get((g, s_ - 2)) or (evacA.get((g - 1, s_ + 2)) if g else None)
                if pv:
                    waits.append((sAC, pv))
                if g and s_ < 2:
                    waits.append((sDV, op_dv[(g - 1, 3, 3)]))
                if g and s_ == 0:
                    waits.append((sPE, qk_pe[(g - 1, NH - 1)]))
                def fnT(sl=sl, pk=pk):
                    def go(eng):
                        return [eng.transpose(bank_b(pk, 2)[:, I * 128:(I + 1) * 128],
                                              xnm[sl][:, I * 128:(I + 1) * 128], idb[:])
                                for I in range(NI)]
                    return go
                P.op("tensor", fnT(), waits=waits, incs=[(sPE, 1)])
                def fnE(pk=pk, s_=s_):
                    def go(eng):
                        dstp = bass.AP(xnT, s_ * 128, [[NI * QG, 128], [QG, NI], [1, 128]])
                        return eng.copy(dstp, bank_b(pk, 2))
                    return go
                P.op("scalar", fnE(), waits=[(sPE, sPE.count)], incs=[(sAC, 1)])
                evacA[(g, s_)] = sAC.count
            xnT_ready = sAC.count

            def ld_wqk(hh, g=g):
                slh = hh % 2
                waits = []
                pv = qk_pe.get((g, hh - 2)) or (qk_pe.get((g - 1, hh + NH - 2)) if g else None)
                if pv:
                    waits.append((sPE, pv))
                pvl = wqk_ld.get((g, hh - 2)) or (wqk_ld.get((g - 1, hh + NH - 2)) if g else None)
                if pvl:
                    waits.append((sWQK[slh], pvl))
                if ("qk", hh) in prep_ready:
                    waits.append((sems["prep"], prep_ready[("qk", hh)]))
                P.op("sync", (lambda hh=hh, slh=slh: lambda eng: eng.dma_start(wqk[slh][:], s_qk[hh]))(),
                     waits=waits, incs=[(sWQK[slh], 16)])
                wqk_ld[(g, hh)] = sWQK[slh].count

            def ld_khist(hh, g=g):
                if g == 0:
                    return
                par = hh % 2
                waits = [(sKW, sKW.count)]
                pv = scores_pe.get((g, hh - 2, 3)) or scores_pe.get((g - 1, hh + NH - 2, 3))
                if pv:
                    waits.append((sPE, pv))
                pvh = hist_kh.get((g, hh - 2)) or hist_kh.get((g - 1, hh + NH - 2))
                if pvh:
                    waits.append((sKH[par], pvh))
                P.op("sync",
                     (lambda hh=hh, g=g, par=par:
                      lambda eng: eng.dma_start(kwk[par][:, 0:g * QG],
                                                bass.AP(k_scr.tensor, (seq * NH + hh) * 128 * S,
                                                        [[S, 128], [1, g * QG]])))(),
                     waits=waits, incs=[(sKH[par], 16)])
                hist_kh[(g, hh)] = sKH[par].count

            def ld_vhist(hh, g=g):
                if g == 0:
                    return
                par = hh % 2
                waits = [(sVW, sVW.count)]
                pv = pv_pe.get((g, hh - 2)) or pv_pe.get((g - 1, hh + NH - 2))
                if pv:
                    waits.append((sPE, pv))
                pvh = hist_vh.get((g, hh - 2)) or hist_vh.get((g - 1, hh + NH - 2))
                if pvh:
                    waits.append((sVH[par], pvh))
                P.op("sync",
                     (lambda hh=hh, g=g, par=par:
                      lambda eng: eng.dma_start(vwk[par][:, 0:g * 512],
                                                bass.AP(v_scr.tensor,
                                                        (seq * NH + hh) * (S // 128) * 128 * 128,
                                                        [[128, 128], [128 * 128, g * 4], [1, 128]])))(),
                     waits=waits, incs=[(sVH[par], 16)])
                hist_vh[(g, hh)] = sVH[par].count

            def ld_wrt(j, g=g):
                waits = []
                pv = rot_pe.get((g, j - 1)) or (rot_pe.get((g - 1, j - 1 + NH // 2)) if g else None)
                if pv:
                    waits.append((sPE, pv))
                pvl = wrt_ld.get((g, j - 1)) or (wrt_ld.get((g - 1, NH // 2 - 1)) if g else None)
                if pvl:
                    waits.append((sWRT, pvl))
                if ("rot", j, 1, 1) in prep_ready:
                    waits.append((sems["prep"], prep_ready[("rot", j, 1, 1)]))
                P.op("sync", (lambda j=j: lambda eng: eng.dma_start(wrtb[:], s_rot[j]))(),
                     waits=waits, incs=[(sWRT, 16)])
                wrt_ld[(g, j)] = sWRT.count

            def ld_wv(s4, g=g):
                waits = []
                pv = vprojE.get((g, s4 - 1)) or (vprojE.get((g - 1, 3)) if g else None)
                if pv:
                    waits.append((sPE, pv))
                pvl = wv_ld.get((g, s4 - 1)) or (wv_ld.get((g - 1, 3)) if g else None)
                if pvl:
                    waits.append((sWV, pvl))
                if ("wv", s4, 3) in prep_ready:
                    waits.append((sems["prep"], prep_ready[("wv", s4, 3)]))
                P.op("sync", (lambda s4=s4: lambda eng: eng.dma_start(wvb[:], s_wv[s4]))(),
                     waits=waits, incs=[(sWV, 16)])
                wv_ld[(g, s4)] = sWV.count

            def emit_rotp(j, which, g=g):
                waits = [(sWRT, wrt_ld[(g, j)]), (sAC, xnT_ready)]
                if which == 0:
                    if last_vev[0]:
                        waits.append((sDV, last_vev[0]))
                    pvr = rot_evac.get((g, j - 1, 1)) or \
                        (rot_evac.get((g - 1, NH // 2 - 1, 1)) if g else None)
                    if pvr:
                        waits.append((sAC, pvr))
                else:
                    waits.append((sAC, rot_evac[(g, j, 0)]))
                def fnR(which=which):
                    def go(eng):
                        return [eng.matmul(bank_f(3)[0:64, :],
                                           wrtb[:, i * 128 + which * 64: i * 128 + (which + 1) * 64],
                                           xnT[:, i * QG:(i + 1) * QG],
                                           start=(i == 0), stop=(i == NI - 1))
                                for i in range(NI)]
                    return go
                P.op("tensor", fnR(), waits=waits, incs=[(sPE, 1)])
                if which == 1:
                    rot_pe[(g, j)] = sPE.count
                ew = [(sPE, sPE.count)]
                pvd = rope_dv.get((g, 2 * j - 3)) or \
                    (rope_dv.get((g - 1, 2 * j - 3 + NH)) if g else None)
                if pvd:
                    ew.append((sDV, pvd))
                P.op("scalar",
                     (lambda which=which, j=j: lambda eng: eng.copy(
                         rotbuf[j % 2][:, which * 512:(which + 1) * 512], bank_f(3)[0:64, :]))(),
                     waits=ew, incs=[(sAC, 1)])
                rot_evac[(g, j, which)] = sAC.count

            def emit_vproj(s4, g=g):
                sv4 = s4 % 2
                for s_ in range(4):
                    waits = [(sWV, wv_ld[(g, s4)]), (sAC, xnT_ready),
                             (sAC, rot_evac[(g, 2 * s4, 1)])]
                    if s_ > 0:
                        waits.append((sDV, sDV.count))
                    key = (g, s4 - 2) if s4 >= 2 else ((g - 1, s4 + 2) if g else None)
                    if key and (key[0], key[1] * 4 + 3) in vcur_d:
                        waits.append((sDV, vcur_d[(key[0], key[1] * 4 + 3)]))
                        waits.append((sVW, sVW.count))
                    def fnV(s_=s_):
                        def go(eng):
                            return [eng.matmul(bank_f(3),
                                               xnT[:, i * QG + s_ * 128: i * QG + (s_ + 1) * 128],
                                               wvb[:, i * 512:(i + 1) * 512],
                                               start=(i == 0), stop=(i == NI - 1))
                                    for i in range(NI)]
                        return go
                    P.op("tensor", fnV(), waits=waits, incs=[(sPE, 1)])
                    vprojE[(g, s4)] = sPE.count
                    P.op("vector",
                         (lambda sv4=sv4, s_=s_, s4=s4: lambda eng: eng.scalar_tensor_tensor(
                             v4[sv4][:, s_ * 512:(s_ + 1) * 512], bank_f(3), 0.0,
                             bvbb[:, s4 * 512:(s4 + 1) * 512], ALU.add, ALU.add))(),
                         waits=[(sPE, sPE.count)], incs=[(sDV, 1)])
                vev_d[(g, s4)] = sDV.count
                last_vev[0] = sDV.count
                for hh4 in range(4):
                    P.op("gpsimd",
                         (lambda s4=s4, hh4=hh4, sv4=sv4, g=g:
                          lambda eng: eng.dma_start(
                              bass.AP(v_scr.tensor,
                                      ((seq * NH + s4 * 4 + hh4) * (S // 128) + g * 4) * 128 * 128,
                                      [[128, 128], [128 * 128, 4], [1, 128]]),
                              bass.AP(v4[sv4], hh4 * 128, [[2048, 128], [512, 4], [1, 128]])))(),
                         waits=[(sDV, vev_d[(g, s4)])], incs=[(sVW, 16)])

            def emit_qk(hh, which, g=g):
                slh = hh % 2
                waits = [(sWQK[slh], wqk_ld[(g, hh)]), (sAC, xnT_ready)]
                pvd = rope_dv.get((g, hh - 1)) or (rope_dv.get((g - 1, NH - 1)) if g else None)
                pva = rope_ac_d.get((g, hh - 1)) or (rope_ac_d.get((g - 1, NH - 1)) if g else None)
                if pvd:
                    waits.append((sDV, pvd))
                if pva:
                    waits.append((sAC, pva))
                def fnQ(slh=slh, which=which):
                    def go(eng):
                        return [eng.matmul(bank_f(which),
                                           wqk[slh][:, i * 256 + which * 128: i * 256 + (which + 1) * 128],
                                           xnT[:, i * QG:(i + 1) * QG],
                                           start=(i == 0), stop=(i == NI - 1))
                                for i in range(NI)]
                    return go
                P.op("tensor", fnQ(), waits=waits, incs=[(sPE, 1)])
                if which == 1:
                    qk_pe[(g, hh)] = sPE.count

            def emit_rope(hh, g=g, pos0=pos0):
                e = hh % 2
                jp = hh // 2
                par = hh % 2
                lo, hi = e * 32, e * 32 + 32
                slot_war = scores_pe.get((g, hh - 2, 3)) or \
                    (scores_pe.get((g - 1, hh + NH - 2, 3)) if g else None)
                for which, (bt, brt, dst_ap) in enumerate((
                        (bqb, brqb, qwk[par][:]),
                        (bkb, brkb, kwk[par][:, pos0:pos0 + QG]))):
                    w0 = [(sPE, qk_pe[(g, hh)])]
                    if slot_war:
                        w0.append((sPE, slot_war))
                    if which == 1:
                        w0.append((sDV, sDV.count))
                        kwpv = kw_d.get((g, hh - 2)) or \
                            (kw_d.get((g - 1, hh + NH - 2)) if g else None)
                        if kwpv:
                            w0.append((sKW, kwpv))
                    P.op("vector",
                         (lambda which=which, bt=bt, hh=hh, lo=lo, hi=hi, pos0=pos0:
                          lambda eng: eng.scalar_tensor_tensor(
                              tmp1[lo:hi, :], bank_f(which)[lo:hi, :], bt[lo:hi, hh:hh + 1],
                              cosb[lo:hi, pos0:pos0 + QG], ALU.add, ALU.mult))(),
                         waits=w0)
                    P.op("vector",
                         (lambda which=which, brt=brt, jp=jp, lo=lo, hi=hi, pos0=pos0:
                          lambda eng: eng.scalar_tensor_tensor(
                              tmp2[lo:hi, :], rotbuf[jp % 2][lo:hi, which * 512:(which + 1) * 512],
                              brt[lo:hi, jp:jp + 1],
                              sinb[lo:hi, pos0:pos0 + QG], ALU.add, ALU.mult))(),
                         waits=[(sAC, rot_evac[(g, jp, which)])])
                    P.op("vector",
                         (lambda dst_ap=dst_ap, lo=lo, hi=hi:
                          lambda eng: eng.tensor_add(dst_ap[lo:hi, :], tmp1[lo:hi, :], tmp2[lo:hi, :]))(),
                         waits=[(sDV, sDV.count)])
                    if e == 0:
                        spans = ((32, 64), (64, 128))
                    else:
                        spans = ((0, 32), (64, 128))
                    for si, (p0, p1) in enumerate(spans):
                        P.op("scalar",
                             (lambda dst_ap=dst_ap, which=which, bt=bt, hh=hh, p0=p0, p1=p1:
                              lambda eng: eng.activation(
                                  dst_ap[p0:p1, :], bank_f(which)[p0:p1, :], AF.Identity,
                                  bias=bt[p0:p1, hh:hh + 1]))(),
                             waits=(w0 if si == 0 else ()),
                             incs=([(sAC, 1)] if si == 1 else ()))
                rope_dv[(g, hh)] = sDV.count
                rope_ac_d[(g, hh)] = sAC.count
                P.op("gpsimd",
                     (lambda hh=hh, pos0=pos0, par=par:
                      lambda eng: eng.dma_start(
                          bass.AP(k_scr.tensor, (seq * NH + hh) * 128 * S + pos0, [[S, 128], [1, QG]]),
                          kwk[par][:, pos0:pos0 + QG]))(),
                     waits=[(sDV, rope_dv[(g, hh)]), (sAC, rope_ac_d[(g, hh)]),
                            (sKW, sKW.count)], incs=[(sKW, 16)])
                kw_d[(g, hh)] = sKW.count

            def emit_vcur(hh, g=g):
                s4 = hh // 4
                sv4c = s4 % 2
                par = hh % 2
                waits = [(sDV, vev_d[(g, s4)])]
                pv = pv_pe.get((g, hh - 2)) or (pv_pe.get((g - 1, hh + NH - 2)) if g else None)
                if pv:
                    waits.append((sPE, pv))
                P.op("vector",
                     (lambda hh=hh, g=g, sv4c=sv4c, par=par: lambda eng: eng.tensor_copy(
                         bass.AP(vwk[par], g * 512, [[S, 128], [128, 4], [1, 128]]),
                         bass.AP(v4[sv4c], (hh % 4) * 128, [[2048, 128], [512, 4], [1, 128]])))(),
                     waits=waits, incs=[(sDV, 1)])
                vcur_d[(g, hh)] = sDV.count

            ld_wqk(0)
            ld_khist(0)
            ld_vhist(0)
            ld_wrt(0)
            ld_wv(0)
            ld_wqk(1)
            ld_khist(1)
            ld_vhist(1)
            emit_rotp(0, 0)
            emit_rotp(0, 1)
            emit_vproj(0)
            emit_qk(0, 0)
            emit_qk(0, 1)
            emit_rope(0)
            emit_vcur(0)
            ld_wrt(1)

            for h in range(NH):
                par = h % 2

                def emit_scores(t, h=h, par=par, g=g):
                    nkb = 4 * g + t + 1
                    keys = nkb * 128
                    waits = [(sDV, rope_dv[(g, h)]), (sAC, rope_ac_d[(g, h)])]
                    if t == 0:
                        waits.append((sAC, xnT_ready))
                        if g > 0:
                            waits.append((sKH[par], hist_kh[(g, h)]))
                    pv = exp_ac.get((g, h, t - 1)) or (exp_ac.get((g, h - 1, 3)) if h else None)
                    if pv:
                        waits.append((sAC, pv))
                    def fnS2(keys=keys, t=t):
                        def go(eng):
                            return [eng.matmul(pp[:, 2048 + n0:2048 + min(n0 + 512, keys)],
                                               qwk[par][:, t * 128:(t + 1) * 128],
                                               kwk[par][:, n0:min(n0 + 512, keys)],
                                               start=True, stop=True)
                                    for n0 in range(0, keys, 512)]
                        return go
                    P.op("tensor", fnS2(), waits=waits, incs=[(sPE, 1)])
                    scores_pe[(g, h, t)] = sPE.count

                def emit_chain(t, h=h, par=par, g=g):
                    nkb = 4 * g + t + 1
                    keys = nkb * 128
                    P.op("vector",
                         (lambda nkb=nkb: lambda eng: eng.tensor_tensor(
                             pp[:, 2048 + (nkb - 1) * 128: 2048 + nkb * 128],
                             pp[:, 2048 + (nkb - 1) * 128: 2048 + nkb * 128],
                             maskb[:], ALU.add))(),
                         waits=[(sPE, scores_pe[(g, h, t)])])
                    P.op("vector",
                         (lambda keys=keys: lambda eng: eng.reduce_max(
                             negmax, pp[:, 2048:2048 + keys], axis=AX.X, negate=True))(),
                         waits=[(sDV, sDV.count)])
                    slp = t % 2
                    waits = [(sDV, sDV.count)]
                    pv = tpos_pe.get((g, h, t - 2)) or (tpos_pe.get((g, h - 1, t + 2)) if h else
                                                        (tpos_pe.get((g - 1, NH - 1, t + 2)) if g else None))
                    if pv:
                        waits.append((sPE, pv))
                    P.op("scalar",
                         (lambda slp=slp, keys=keys: lambda eng: eng.activation(
                             psb[slp][:, 0:keys], pp[:, 2048:2048 + keys], AF.Exp,
                             bias=negmax, accum_out=den[:, 0:1]))(),
                         waits=waits, incs=[(sAC, 1)])
                    exp_ac[(g, h, t)] = sAC.count
                    P.op("vector", lambda eng: eng.reciprocal(den[:, 1:2], den[:, 0:1]),
                         waits=[(sAC, sAC.count)])
                    P.op("vector",
                         (lambda slp=slp, keys=keys: lambda eng: eng.tensor_scalar_mul(
                             psb[slp][:, 0:keys], psb[slp][:, 0:keys], den[:, 1:2]))(),
                         waits=[(sDV, sDV.count)])
                    pscale_d[(g, h, t)] = sDV.count

                def emit_pt(t, h=h, g=g):
                    nkb = 4 * g + t + 1
                    slp = t % 2
                    for r0 in range(0, nkb, 8):
                        r1 = min(r0 + 8, nkb)
                        waits = [(sDV, pscale_d[(g, h, t)]), (sAC, ptev_last[0])]
                        if t == 0 and r0 == 0:
                            pv = attn_dv.get((g, h - 1)) or (attn_dv.get((g - 1, NH - 1)) if g else None)
                            if pv:
                                waits.append((sDV, pv))
                        def fnPT(slp=slp, r0=r0, r1=r1):
                            def go(eng):
                                return [eng.transpose(bank_b(2)[:, (kb - r0) * 128:(kb - r0 + 1) * 128],
                                                      psb[slp][:, kb * 128:(kb + 1) * 128], idb[:])
                                        for kb in range(r0, r1)]
                            return go
                        P.op("tensor", fnPT(), waits=waits, incs=[(sPE, 1)])
                        tpos_pe[(g, h, t)] = sPE.count
                        def fnPTE(r0=r0, r1=r1, t=t):
                            def go(eng):
                                dstp = bass.AP(ptb, r0 * 512 + t * 128,
                                               [[16 * 512, 128], [512, r1 - r0], [1, 128]])
                                return eng.copy(dstp, bank_b(2)[:, 0:(r1 - r0) * 128])
                            return go
                        P.op("scalar", fnPTE(), waits=[(sPE, sPE.count)], incs=[(sAC, 1)])
                        ptev_last[0] = sAC.count

                def emit_pv(hp, g=g):
                    parp = hp % 2
                    waits = [(sAC, pt_ac[(g, hp)]), (sDV, vcur_d[(g, hp)])]
                    if g > 0:
                        waits.append((sVH[parp], hist_vh[(g, hp)]))
                    def fnPV(parp=parp):
                        def go(eng):
                            r = []
                            nkb = 4 * g + 4
                            for kb in range(nkb):
                                qs = max(0, (kb - 4 * g) * 128)
                                r.append(eng.matmul(bank_f(2)[:, qs:512],
                                                    vwk[parp][:, kb * 128:(kb + 1) * 128],
                                                    ptb[:, kb * 512 + qs:(kb + 1) * 512],
                                                    start=(kb == 0), stop=(kb == nkb - 1)))
                            return r
                        return go
                    P.op("tensor", fnPV(), waits=waits, incs=[(sPE, 1)])
                    pv_pe[(g, hp)] = sPE.count
                    P.op("vector",
                         (lambda hp=hp: lambda eng: eng.tensor_copy(
                             afm[:, hp * 512:(hp + 1) * 512], bank_f(2)))(),
                         waits=[(sPE, pv_pe[(g, hp)])], incs=[(sDV, 1)])
                    attn_dv[(g, hp)] = sDV.count

                emit_scores(0)
                emit_chain(0)
                if h > 0:
                    emit_pv(h - 1)
                    ld_vhist(h + 1) if h + 1 < NH else None
                if h + 1 < NH:
                    emit_qk(h + 1, 0)
                emit_scores(1)
                emit_chain(1)
                emit_pt(0)
                if h + 1 < NH:
                    emit_qk(h + 1, 1)
                emit_scores(2)
                emit_chain(2)
                emit_pt(1)
                if h % 2 == 0 and h // 2 + 1 <= NH // 2 - 1:
                    emit_rotp(h // 2 + 1, 0)
                elif h % 2 == 1 and h + 2 < NH:
                    emit_rotp((h + 1) // 2, 1)
                emit_scores(3)
                emit_chain(3)
                emit_pt(2)
                if h % 4 == 3 and h + 1 < NH:
                    emit_vproj(h // 4 + 1)
                if h + 1 < NH:
                    emit_rope(h + 1)
                    emit_vcur(h + 1)
                emit_pt(3)
                pt_ac[(g, h)] = ptev_last[0]
                if h == NH - 1:
                    emit_pv(h)

                if h + 2 < NH:
                    ld_wqk(h + 2)
                    ld_khist(h + 2)
                if h % 2 == 1 and (h + 3) // 2 <= NH // 2 - 1:
                    ld_wrt((h + 3) // 2)
                if h % 4 == 1 and h // 4 + 1 <= 3:
                    ld_wv(h // 4 + 1)

            for gf in range(4):
                slo = 0
                waits = []
                pv = op_pe.get((g, gf - 1)) or (op_pe.get((g - 1, 3)) if g else None)
                if pv:
                    waits.append((sPE, pv))
                if gf == 0:
                    waits.append((sPE, pv_pe[(g, NH - 1)]))
                    waits.append((sAC, pt_ac[(g, NH - 1)]))
                if ("wo", gf) in prep_ready:
                    waits.append((sems["prep"], prep_ready[("wo", gf)]))
                P.op("sync", (lambda gf=gf, slo=slo: lambda eng: eng.dma_start(wob[:], s_wo[gf]))(),
                     waits=waits, incs=[(sWO[slo], 16)])
                for s_ in range(4):
                    bko = 4 + (gf * 4 + s_) % 4
                    waits = [(sWO[slo], sWO[slo].count), (sDV, attn_dv[(g, NH - 1)]),
                             (sAC, exp_ac[(g, NH - 1, 3)])]
                    pv = op_dv.get((g, gf - 1, s_)) or (op_dv.get((g - 1, 3, s_)) if g else None)
                    if pv:
                        waits.append((sDV, pv))
                    def fnO(s_=s_, bko=bko):
                        def go(eng):
                            return [eng.matmul(bank_f(bko),
                                               afm[:, hh * 512 + s_ * 128: hh * 512 + (s_ + 1) * 128],
                                               wob[:, hh * 512:(hh + 1) * 512],
                                               start=(hh == 0), stop=(hh == NH - 1))
                                    for hh in range(NH)]
                        return go
                    P.op("tensor", fnO(), waits=waits, incs=[(sPE, 1)])
                    op_pe[(g, gf)] = sPE.count
                    wot = [(sPE, sPE.count)]
                    pvo = op_dv.get((g, gf, s_ - 1)) or (op_dv.get((g, gf - 1, 3)) if gf else None)
                    if pvo:
                        wot.append((sDV, pvo))
                    P.op("vector",
                         (lambda bko=bko, gf=gf: lambda eng: eng.tensor_tensor(
                             otmp[:], bank_f(bko), bobb[:, gf * 512:(gf + 1) * 512], ALU.add))(),
                         waits=wot)
                    P.op("vector",
                         (lambda s_=s_, gf=gf: lambda eng: eng.tensor_add(
                             x1g[:, s_ * H + gf * 512: s_ * H + (gf + 1) * 512],
                             otmp[:],
                             x1g[:, s_ * H + gf * 512: s_ * H + (gf + 1) * 512]))(),
                         waits=[(sDV, sDV.count)])
                    op_dv[(g, gf, s_)] = sDV.count
            for s_ in range(4):
                waits = [(sDV, max(op_dv[(g, gf, s_)] for gf in range(4)))]
                P.op("gpsimd",
                     (lambda s_=s_, grow=grow: lambda eng: eng.dma_start(
                         x2s[grow + s_ * 128:grow + (s_ + 1) * 128, :], x1g[:, s_ * H:(s_ + 1) * H]))(),
                     waits=waits, incs=[(sX2, 16)])
        P.op("gpsimd", lambda eng: eng.nop(),
             waits=[(sX2, sX2.count), (sKW, sKW.count), (sVW, sVW.count)])
        drains = [(s, s.count) for s in
                  [sCst, sWRT, sWV] + sKH + sVH + sX1 + sWQK + sWO]
        P.op("sync", lambda eng: eng.nop(), waits=drains)


    if "f1" in phases:
        with contextlib.ExitStack() as ph:
            P1 = Prog(nc, auto=auto_map)
            phase_prologue(P1)
            sPR = emit_prep(P1) if "prep" in phases else None
            emit_ffn(P1, ph, x_in, x1s, "f1", sPR, prep_ready)
            P1.emit()
    for seq in range(SPC):
        if f"a{seq}" in phases:
            with contextlib.ExitStack() as ph:
                Pa = Prog(nc, auto=auto_map)
                phase_prologue(Pa)
                if seq == 0 and prep_ready.get("_batch2"):
                    sPR = sems["prep"]
                    for key, src_, dst_ in prep_ready.pop("_batch2"):
                        Pa.op("gpsimd",
                              (lambda src_=src_, dst_=dst_:
                               lambda eng: eng.dma_start(dst_, src_))(),
                              waits=[(sPR, sPR.count)], incs=[(sPR, 16)])
                        prep_ready[key] = sPR.count
                emit_attn(Pa, ph, seq, f"a{seq}")
                Pa.emit()
    if "f3" in phases:
        with contextlib.ExitStack() as ph:
            P3 = Prog(nc, auto=auto_map)
            phase_prologue(P3)
            emit_ffn(P3, ph, x2s, out, "f3", None, prep_ready)
            P3.emit()
    if "f3" not in phases:
        with (nc.sbuf_tensor("zzz", [128, 128], F32) as zz,
              nc.semaphore("zzs") as zzs, nc.semaphore("zzd") as zzd,
              nc.Block() as blk):
            @blk.vector
            def _(ve):
                ve.memset(zz[:], 0.0).then_inc(zzs, 1)
            @blk.gpsimd
            def _(gp):
                gp.wait_ge(zzs, 1)
                gp.dma_start(out[0:128, 0:128], zz[:]).then_inc(zzd, 16)
                gp.wait_ge(zzd, 16)
    st.close()
    return nc


def kernel(L0_attn_hidden_states, L0_attn_w_qkv, L0_attn_b_qkv, L0_attn_w_out,
           L0_attn_b_out, L0_ffn_w_gate, L0_ffn_w_up, L0_ffn_w_down):
    x = np.asarray(L0_attn_hidden_states, dtype=np.float32)
    wqkv = np.asarray(L0_attn_w_qkv, dtype=np.float32)
    bqkv = np.asarray(L0_attn_b_qkv, dtype=np.float32)
    wout = np.asarray(L0_attn_w_out, dtype=np.float32)
    bout = np.asarray(L0_attn_b_out, dtype=np.float32)

    wqkvp = _host_prep_wqkv(wqkv)
    wg_t = np.ascontiguousarray(np.asarray(L0_ffn_w_gate, np.float32).T)
    wu_t = np.ascontiguousarray(np.asarray(L0_ffn_w_up, np.float32).T)
    wd_t = np.ascontiguousarray(np.asarray(L0_ffn_w_down, np.float32).T)
    cosp, sinp, idbf, idf32, mask = _host_tables()

    bq3 = bqkv.reshape(NH, 3 * HD)
    op = np.array(_odd_perm())
    bq = np.zeros((128, NH), np.float32)
    bk = np.zeros((128, NH), np.float32)
    for h in range(NH):
        qb, kb_ = bq3[h, 0:128], bq3[h, 128:256]
        if h % 2 == 1:
            qb, kb_ = qb[op], kb_[op]
        bq[:, h], bk[:, h] = qb, kb_
    perm = np.array([(m + 16) % 32 for m in range(32)])
    brq = np.zeros((64, NH // 2), np.float32)
    brk = np.zeros((64, NH // 2), np.float32)
    for jp in range(NH // 2):
        for e in range(2):
            brq[e * 32:(e + 1) * 32, jp] = bq3[2 * jp + e, perm]
            brk[e * 32:(e + 1) * 32, jp] = bq3[2 * jp + e, 128 + perm]
    bv = bq3[:, 256:384].reshape(-1)
    bvb = np.ascontiguousarray(np.broadcast_to(bv, (128, H))).astype(np.float32)
    bob = np.ascontiguousarray(np.broadcast_to(bout, (128, H))).astype(np.float32)

    nc = build_program()
    shared = dict(wg_t=wg_t, wu_t=wu_t, wd_t=wd_t, wqkvp=wqkvp, wo=wout,
                  t_cos=cosp, t_sin=sinp, t_idb=idbf, t_idf=idf32, t_mask=mask,
                  t_bq=bq, t_bk=bk, t_brq=brq, t_brk=brk, t_bvb=bvb, t_bob=bob)
    in_maps = []
    for c in range(NCORE):
        m = dict(shared)
        m["x"] = np.ascontiguousarray(x[c * SPC:(c + 1) * SPC].reshape(TOK, H))
        in_maps.append(m)
    res = run_bass_kernel_spmd(nc, in_maps, core_ids=list(range(NCORE)))
    outs = [r["out"].reshape(SPC, S, H) for r in res.results]
    return np.concatenate(outs, axis=0)


# version 07
# speedup vs baseline: 1.1017x; 1.1017x over previous
import numpy as np
import ml_dtypes
import contextlib
import concourse.bass as bass
import concourse.mybir as mybir
from concourse.bass_utils import run_bass_kernel_spmd

F32 = mybir.dt.float32
BF16 = mybir.dt.bfloat16
AF = mybir.ActivationFunctionType
ALU = mybir.AluOpType
AX = mybir.AxisListType

NCORE = 8
B, S, H = 16, 2048, 2048
NH, HD = 16, 128
FF = 8192
SPC = B // NCORE
TOK = SPC * S
TP = 512
NPASS = TOK // TP
QG = 512
NG = S // QG
NI = H // 128
NHC = FF // 128
NFO = H // 128
EPS = 1e-5

ENGINES = ("sync", "gpsimd", "tensor", "vector", "scalar")


class Sem:
    def __init__(self, nc, stack, name):
        self.h = stack.enter_context(nc.semaphore(name))
        self.name = name
        self.count = 0


class Prog:

    def __init__(self, nc, auto=None):
        self.nc = nc
        self.ops = {e: [] for e in ENGINES}
        self.wm = {e: {} for e in ENGINES}
        self.auto = auto or {}

    def op(self, engine, fn, waits=(), incs=(), inc_each=False):
        eff = []
        for s, v in waits:
            if v is None or v <= 0:
                continue
            assert v <= s.count, f"wait on future of {s.name}: {v} > {s.count}"
            if self.wm[engine].get(s.name, 0) < v:
                self.wm[engine][s.name] = v
                eff.append((s, v))
        incs = list(incs)
        if engine in self.auto and not any(s is self.auto[engine] for s, _ in incs):
            incs.append((self.auto[engine], 1))
        swv = None
        for s, k in incs:
            s.count += 2 * k if engine == "tensor" else k
        if engine == "tensor" and incs:
            s0, k0 = incs[0]
            swv = (s0, s0.count - k0)
        self.ops[engine].append((eff, fn, incs, inc_each, swv))

    def emit(self):
        with self.nc.Block() as blk:
            for ename in ENGINES:
                ops = self.ops[ename]
                if not ops:
                    continue

                def make(ops, ename=ename):
                    def run(eng):
                        for waits, fn, incs, inc_each, swv in ops:
                            for s, v in waits:
                                eng.wait_ge(s.h, v)
                            ret = fn(eng)
                            if incs:
                                insts = ret if isinstance(ret, list) else [ret]
                                if ename == "tensor":
                                    last = insts[-1]
                                    for s, k in incs:
                                        last.then_inc(s.h, k)
                                    eng.wait_ge(swv[0].h, swv[1])
                                    nop = eng.nop()
                                    for s, k in incs:
                                        nop.then_inc(s.h, k)
                                else:
                                    tgt = insts if inc_each else insts[-1:]
                                    for inst in tgt:
                                        for s, k in incs:
                                            inst.then_inc(s.h, k)
                    return run

                getattr(blk, ename)(make(ops))


def _odd_perm():
    return list(range(32, 64)) + list(range(0, 32)) + list(range(64, 128))


def _host_prep_wqkv(wqkv):
    cols = []
    op = _odd_perm()
    for h in range(NH):
        base = h * 3 * HD
        q = list(range(base, base + 128))
        k = list(range(base + 128, base + 256))
        if h % 2 == 1:
            q = [q[i] for i in op]
            k = [k[i] for i in op]
        rotq = [base + ((m + 16) % 32) for m in range(32)]
        rotk = [base + 128 + ((m + 16) % 32) for m in range(32)]
        v = list(range(base + 256, base + 384))
        cols += q + k + rotq + rotk + v
    return np.ascontiguousarray(wqkv[:, cols])


def _host_tables():
    j = np.arange(16, dtype=np.float64)
    invf = 10000.0 ** (-j / 16.0)
    t = np.arange(S, dtype=np.float64)
    ang = t[None, :] * invf[:, None]
    cos16, sin16 = np.cos(ang), np.sin(ang)
    cos32 = np.concatenate([cos16, cos16], 0)
    sin32 = np.concatenate([-sin16, sin16], 0)
    cosp = np.concatenate([cos32, cos32], 0).astype(np.float32)
    sinp = np.concatenate([sin32, sin32], 0).astype(np.float32)
    idbf = np.eye(128, dtype=ml_dtypes.bfloat16)
    idf32 = np.eye(128, dtype=np.float32)
    mask = np.where(np.tril(np.ones((128, 128), bool)), 0.0, -30000.0).astype(np.float32)
    return cosp, sinp, idbf, idf32, mask


def build_program(debug=False, phases=("prep", "f1", "a0", "a1", "f3")):
    nc = bass.Bass("TRN2", target_bir_lowering=False)
    st = contextlib.ExitStack()

    x_in = nc.dram_tensor("x", [TOK, H], F32, kind="ExternalInput").ap()
    wg_t = nc.dram_tensor("wg_t", [H, FF], F32, kind="ExternalInput").ap()
    wu_t = nc.dram_tensor("wu_t", [H, FF], F32, kind="ExternalInput").ap()
    wd_t = nc.dram_tensor("wd_t", [FF, H], F32, kind="ExternalInput").ap()
    wqkv = nc.dram_tensor("wqkvp", [H, NH * 448], F32, kind="ExternalInput").ap()
    wo = nc.dram_tensor("wo", [H, H], F32, kind="ExternalInput").ap()
    t_cos = nc.dram_tensor("t_cos", [64, S], F32, kind="ExternalInput").ap()
    t_sin = nc.dram_tensor("t_sin", [64, S], F32, kind="ExternalInput").ap()
    t_idb = nc.dram_tensor("t_idb", [128, 128], BF16, kind="ExternalInput").ap()
    t_idf = nc.dram_tensor("t_idf", [128, 128], F32, kind="ExternalInput").ap()
    t_mask = nc.dram_tensor("t_mask", [128, 128], F32, kind="ExternalInput").ap()
    t_bq = nc.dram_tensor("t_bq", [128, NH], F32, kind="ExternalInput").ap()
    t_bk = nc.dram_tensor("t_bk", [128, NH], F32, kind="ExternalInput").ap()
    t_brq = nc.dram_tensor("t_brq", [64, NH // 2], F32, kind="ExternalInput").ap()
    t_brk = nc.dram_tensor("t_brk", [64, NH // 2], F32, kind="ExternalInput").ap()
    t_bvb = nc.dram_tensor("t_bvb", [128, H], F32, kind="ExternalInput").ap()
    t_bob = nc.dram_tensor("t_bob", [128, H], F32, kind="ExternalInput").ap()
    out = nc.dram_tensor("out", [TOK, H], F32, kind="ExternalOutput").ap()

    kd = "ExternalOutput" if debug else "Internal"
    x1s = nc.dram_tensor("x1s", [TOK, H], F32, kind=kd).ap()
    x2s = nc.dram_tensor("x2s", [TOK, H], F32, kind=kd).ap()
    s_wg = nc.dram_tensor("s_wg", [NHC, 128, 2048], BF16, kind="Internal").ap()
    s_wu = nc.dram_tensor("s_wu", [NHC, 128, 2048], BF16, kind="Internal").ap()
    s_wd = nc.dram_tensor("s_wd", [NFO, 128, FF], BF16, kind="Internal").ap()
    s_qk = nc.dram_tensor("s_qk", [NH, 128, NI * 256], BF16, kind="Internal").ap()
    s_rot = nc.dram_tensor("s_rot", [NH // 2, 128, NI * 128], BF16, kind="Internal").ap()
    s_wv = nc.dram_tensor("s_wv", [4, 128, NI * 512], BF16, kind="Internal").ap()
    s_wo = nc.dram_tensor("s_wo", [4, 128, NH * 512], BF16, kind="Internal").ap()
    k_scr = nc.dram_tensor("k_scr", [SPC, NH, 128, S], BF16, kind="Internal").ap()
    v_scr = nc.dram_tensor("v_scr", [SPC, NH, S // 128, 128, 128], BF16, kind="Internal").ap()

    sems = {}

    def sem(name):
        if name not in sems:
            sems[name] = Sem(nc, st, name)
        return sems[name]

    sPE, sDV, sAC = sem("sPE"), sem("sDV"), sem("sAC")
    auto_map = {"vector": sDV, "scalar": sAC}

    pp_h = st.enter_context(nc.psum_tensor("pp", [128, 4096], F32))
    pp = pp_h[:]
    pp_bf = pp_h.bitcast(BF16)[:]

    def bank_f(i, n=1):
        return pp[:, i * 512:(i + n) * 512]

    def bank_b(i, n=1):
        return pp_bf[:, i * 1024:(i + n) * 1024]

    cst_h = st.enter_context(nc.sbuf_tensor("cstcol", [128, 2], F32))
    epsb = cst_h[:, 0:1]
    nhalfb = cst_h[:, 1:2]

    def emit_ln(P, stt_, xg, xnm_dst, w_in, war_dv):
        sDV_, sAC_ = sems["sDV"], sems["sAC"]
        ssum, ssq, mvec, msq, var = (stt_[:, i:i + 1] for i in range(5))
        lnv, rstd = stt_[:, 5:6], stt_[:, 6:7]
        w0 = list(w_in)
        if war_dv:
            w0.append((sDV_, war_dv))
        P.op("vector", lambda eng: eng.reduce_sum(ssum, xg, axis=AX.X), waits=w0)
        P.op("scalar",
             lambda eng: eng.activation(xnm_dst, xg, AF.Square, accum_out=ssq),
             waits=w0)
        P.op("vector", lambda eng: eng.tensor_scalar_mul(mvec, ssum, 1.0 / H),
             waits=[(sDV_, sDV_.count)])
        P.op("vector", lambda eng: eng.tensor_mul(msq, mvec, mvec),
             waits=[(sDV_, sDV_.count)])
        P.op("vector", lambda eng: eng.scalar_tensor_tensor(
            var, ssq, 1.0 / H, msq, ALU.mult, ALU.subtract),
             waits=[(sAC_, sAC_.count), (sDV_, sDV_.count)])
        P.op("scalar", lambda eng: eng.activation(lnv, var, AF.Ln, bias=epsb),
             waits=[(sDV_, sDV_.count)])
        P.op("scalar", lambda eng: eng.activation(rstd, lnv, AF.Exp, scale=nhalfb),
             waits=[(sAC_, sAC_.count)])
        P.op("vector",
             (lambda: lambda eng: eng.tensor_scalar(
                 xnm_dst, xg, mvec, rstd, ALU.subtract, ALU.mult))(),
             waits=[(sAC_, sAC_.count)])
        return sDV_.count

    def phase_prologue(P):
        w = [(sPE, sPE.count), (sDV, sDV.count), (sAC, sAC.count)]
        if "prep" in sems:
            w.append((sems["prep"], sems["prep"].count))
        import os as _os5
        dly = int(_os5.environ.get("KDELAY", 0))
        for e in ENGINES:
            if dly:
                P.op(e, (lambda dly=dly: lambda eng: [eng.nop(cycle_cnt=dly) for _ in range(4)])(), waits=list(w))
            else:
                P.op(e, lambda eng: eng.nop(), waits=list(w))

    prep_ready = {}

    def emit_prep(P):
        sPR = sem("prep")
        jobs = []
        for c in range(NHC):
            for wk, srcw, dstw in (("wg", wg_t, s_wg), ("wu", wu_t, s_wu)):
                src = bass.AP(srcw.tensor, c * 128,
                              [[FF, 128], [128 * FF, NI], [1, 128]])
                jobs.append(((wk, c), src, dstw[c]))
        for f in range(NFO):
            src = bass.AP(wd_t.tensor, f * 128,
                          [[H, 128], [128 * H, NHC], [1, 128]])
            jobs.append((("wd", f), src, s_wd[f]))
        for h in range(NH):
            src = bass.AP(wqkv.tensor, h * 448,
                          [[NH * 448, 128], [128 * NH * 448, NI], [1, 256]])
            jobs.append((("qk", h), src, s_qk[h]))
        for jp in range(NH // 2):
            for qk in range(2):
                for e in range(2):
                    src = bass.AP(wqkv.tensor, (2 * jp + e) * 448 + 256 + qk * 32,
                                  [[NH * 448, 128], [128 * NH * 448, NI], [1, 32]])
                    dst = bass.AP(s_rot.tensor, jp * 128 * NI * 128 + qk * 64 + e * 32,
                                  [[NI * 128, 128], [128, NI], [1, 32]])
                    jobs.append((("rot", jp, qk, e), src, dst))
        for s4 in range(4):
            for u in range(4):
                src = bass.AP(wqkv.tensor, (4 * s4 + u) * 448 + 320,
                              [[NH * 448, 128], [128 * NH * 448, NI], [1, 128]])
                dst = bass.AP(s_wv.tensor, s4 * 128 * NI * 512 + u * 128,
                              [[NI * 512, 128], [512, NI], [1, 128]])
                jobs.append((("wv", s4, u), src, dst))
        for g in range(4):
            src = bass.AP(wo.tensor, g * 512,
                          [[H, 128], [128 * H, NH], [1, 512]])
            jobs.append((("wo", g), src, s_wo[g]))

        P.op("vector", lambda eng: eng.memset(epsb, EPS))
        P.op("vector", lambda eng: eng.memset(nhalfb, -0.5), incs=[(sems["sDV"], 1)])
        batch2 = [j for j in jobs if j[0][0] in ("qk", "rot", "wv", "wo")]
        batch1 = [j for j in jobs if j[0][0] not in ("qk", "rot", "wv", "wo")]
        for key, src, dst in batch1:
            P.op("gpsimd",
                 (lambda src=src, dst=dst: lambda eng: eng.dma_start(dst, src))(),
                 waits=[(sPR, sPR.count)], incs=[(sPR, 16)])
            prep_ready[key] = sPR.count
        prep_ready["_batch2"] = batch2
        return sPR

    def emit_ffn(P, st, src, dst, tag, sPR, prep_ready):
        xin = [st.enter_context(nc.sbuf_tensor(f"{tag}_xin{i}", [128, H], F32)) for i in range(2)]
        xnm = [st.enter_context(nc.sbuf_tensor(f"{tag}_xnm{i}", [128, H], BF16)) for i in range(2)]
        xnT = st.enter_context(nc.sbuf_tensor(f"{tag}_xnT", [128, NI * TP], BF16))
        hbuf = st.enter_context(nc.sbuf_tensor(f"{tag}_h", [128, NHC * TP], BF16))
        wgb = [st.enter_context(nc.sbuf_tensor(f"{tag}_wg{i}", [128, 2048], BF16)) for i in range(2)]
        wub = [st.enter_context(nc.sbuf_tensor(f"{tag}_wu{i}", [128, 2048], BF16)) for i in range(2)]
        wdb = [st.enter_context(nc.sbuf_tensor(f"{tag}_wd{i}", [128, 4096], BF16)) for i in range(4)]
        slu = [st.enter_context(nc.sbuf_tensor(f"{tag}_slu{i}", [128, 512], F32)) for i in range(2)]
        fmo = [st.enter_context(nc.sbuf_tensor(f"{tag}_fmo{i}", [128, 512], F32)) for i in range(2)]
        xsl = [st.enter_context(nc.sbuf_tensor(f"{tag}_xsl{i}", [128, 512], F32)) for i in range(2)]
        pcb = [st.enter_context(nc.sbuf_tensor(f"{tag}_pcb{i}", [128, 512], F32)) for i in range(2)]
        idb = st.enter_context(nc.sbuf_tensor(f"{tag}_idb", [128, 128], BF16))
        idf = st.enter_context(nc.sbuf_tensor(f"{tag}_idf", [128, 128], F32))
        stt_ = st.enter_context(nc.sbuf_tensor(f"{tag}_st", [128, 32], F32))

        sXIN = [sem(f"{tag}xin{i}") for i in range(2)]
        sWG = [sem(f"{tag}wg{i}") for i in range(2)]
        sWU = [sem(f"{tag}wu{i}") for i in range(2)]
        sWD = [sem(f"{tag}wd{i}") for i in range(4)]
        sXSL = [sem(f"{tag}xsl{i}") for i in range(2)]
        sPC = [sem(f"{tag}pc{i}") for i in range(2)]
        sID = sem(f"{tag}id")

        P.op("sync", lambda eng: eng.dma_start(idb[:], t_idb), incs=[(sID, 16)])
        P.op("sync", lambda eng: eng.dma_start(idf[:], t_idf), incs=[(sID, 16)])
        id_wm = sID.count

        evacA, gate_pe, up_pe, mult_dv = {}, {}, {}, {}
        down_pe, stt_dv, fmo_dv, xnm_dv = {}, {}, {}, {}

        def prev2(d, p, i, n):
            if i >= 2:
                return d.get((p, i - 2))
            if p >= 1:
                return d.get((p - 1, i + n - 2))
            return None

        xnTr = {}

        def emitA_s(p, s):
            row0 = p * TP
            sl = s % 2
            waits = [(sID, id_wm)]
            pv = prev2(xnm_dv, p, s, 4)
            if pv:
                waits.append((sDV, pv))
            P.op("sync",
                 (lambda s=s, sl=sl, row0=row0:
                  lambda eng: eng.dma_start(xin[sl][:], src[row0 + s * 128:row0 + (s + 1) * 128, :]))(),
                 waits=waits, incs=[(sXIN[sl], 16)])
            w_in = [(sXIN[sl], sXIN[sl].count)]
            war = xnm_dv.get((p, s - 1)) or (xnm_dv.get((p - 1, 3)) if p else None)
            xnm_dv[(p, s)] = emit_ln(P, stt_, xin[sl][:], xnm[sl][:], w_in, war)
            pk = 4 + 2 * sl
            waits = [(sDV, sDV.count)]
            pv = prev2(evacA, p, s, 4)
            if pv:
                waits.append((sAC, pv))
            def fnT(sl=sl, pk=pk):
                def go(eng):
                    return [eng.transpose(bank_b(pk, 2)[:, I * 128:(I + 1) * 128],
                                          xnm[sl][:, I * 128:(I + 1) * 128], idb[:])
                            for I in range(NI)]
                return go
            P.op("tensor", fnT(), waits=waits, incs=[(sPE, 1)])
            def fnE(pk=pk, s=s):
                def go(eng):
                    dstp = bass.AP(xnT, s * 128, [[NI * TP, 128], [TP, NI], [1, 128]])
                    return eng.copy(dstp, bank_b(pk, 2))
                return go
            P.op("scalar", fnE(), waits=[(sPE, sPE.count)], incs=[(sAC, 1)])
            evacA[(p, s)] = sAC.count
            if s == 3:
                xnTr[p] = sAC.count

        for s in range(4):
            emitA_s(0, s)
        for p in range(NPASS):
            row0 = p * TP
            xnT_ready = xnTr[p]

            for c in range(NHC):
                sl = c % 2
                for wk, ring, buf, scr, prevk in (("wg", sWG, wgb, s_wg, gate_pe),
                                                  ("wu", sWU, wub, s_wu, up_pe)):
                    waits = []
                    if sPR is not None:
                        waits.append((sPR, prep_ready[(wk, c)]))
                    pv = prev2(prevk, p, c, NHC)
                    if pv:
                        waits.append((sPE, pv))
                    P.op("sync",
                         (lambda buf=buf, scr=scr, sl=sl, c=c:
                          lambda eng: eng.dma_start(buf[sl][:], scr[c]))(),
                         waits=waits, incs=[(ring[sl], 16)])
                for which, (ring, buf, bk, prevk) in enumerate(
                        ((sWG, wgb, 2 * sl, gate_pe), (sWU, wub, 2 * sl + 1, up_pe))):
                    waits = [(ring[sl], ring[sl].count), (sAC, xnT_ready)]
                    pv = prev2(mult_dv, p, c, NHC)
                    if pv:
                        waits.append((sDV, pv))
                    if p and c < 2 and which == 0:
                        waits.append((sDV, stt_dv[(p - 1, NFO - 1)]))
                    def fnMM(buf=buf, sl=sl, bk=bk):
                        def go(eng):
                            return [eng.matmul(bank_f(bk), buf[sl][:, i * 128:(i + 1) * 128],
                                               xnT[:, i * TP:(i + 1) * TP],
                                               start=(i == 0), stop=(i == NI - 1))
                                    for i in range(NI)]
                        return go
                    P.op("tensor", fnMM(), waits=waits, incs=[(sPE, 1)])
                    prevk[(p, c)] = sPE.count
                waits = [(sPE, gate_pe[(p, c)])]
                pv = prev2(mult_dv, p, c, NHC)
                if pv:
                    waits.append((sDV, pv))
                P.op("scalar",
                     (lambda sl=sl: lambda eng: eng.activation(slu[sl][:], bank_f(2 * sl), AF.Silu))(),
                     waits=waits, incs=[(sAC, 1)])
                P.op("vector",
                     (lambda sl=sl, c=c: lambda eng: eng.tensor_mul(
                         hbuf[:, c * TP:(c + 1) * TP], slu[sl][:], bank_f(2 * sl + 1)))(),
                     waits=[(sAC, sAC.count), (sPE, up_pe[(p, c)])], incs=[(sDV, 1)])
                mult_dv[(p, c)] = sDV.count
            h_ready = sDV.count

            for f in range(NFO + 1):
                if f < NFO:
                    for hf in range(2):
                        hidx = 2 * f + hf
                        slw = hidx % 4
                        waits = []
                        if sPR is not None:
                            waits.append((sPR, prep_ready[("wd", f)]))
                        if hidx >= 4:
                            pvf = f - 2
                            pv = down_pe.get((p, pvf)) if pvf >= 0 else \
                                (down_pe.get((p - 1, pvf + NFO)) if p else None)
                            if pv:
                                waits.append((sPE, pv))
                        P.op("sync",
                             (lambda f=f, hf=hf, slw=slw:
                              lambda eng: eng.dma_start(
                                  wdb[slw][:],
                                  bass.AP(s_wd.tensor, f * 128 * FF + hf * 4096,
                                          [[FF, 128], [1, 4096]])))(),
                             waits=waits, incs=[(sWD[slw], 16)])
                    sl = f % 2
                    waits = []
                    pv = prev2(stt_dv, p, f, NFO)
                    if pv:
                        waits.append((sDV, pv))
                    P.op("sync",
                         (lambda f=f, sl=sl, row0=row0:
                          lambda eng: eng.dma_start(
                              xsl[sl][:].rearrange("p (s m) -> p s m", s=4),
                              bass.AP(src.tensor, row0 * H + f * 128,
                                      [[H, 128], [128 * H, 4], [1, 128]])))(),
                         waits=waits, incs=[(sXSL[sl], 16)])
                    for hf in range(2):
                        slw = (2 * f + hf) % 4
                        waits = [(sWD[slw], sWD[slw].count), (sDV, h_ready)]
                        if hf == 0:
                            pv = prev2(fmo_dv, p, f, NFO)
                            if pv:
                                waits.append((sDV, pv))
                            if p == 0 and f < 2:
                                pass
                            if f < 2:
                                pass
                        def fnD(sl=sl, slw=slw, hf=hf):
                            def go(eng):
                                return [eng.matmul(bank_f(sl), wdb[slw][:, cc * 128:(cc + 1) * 128],
                                                   hbuf[:, (hf * 32 + cc) * TP:(hf * 32 + cc + 1) * TP],
                                                   start=(hf == 0 and cc == 0),
                                                   stop=(hf == 1 and cc == 31))
                                        for cc in range(32)]
                            return go
                        P.op("tensor", fnD(), waits=waits, incs=[(sPE, 1)])
                    down_pe[(p, f)] = sPE.count
                    P.op("vector",
                         (lambda sl=sl: lambda eng: eng.tensor_copy(fmo[sl][:], bank_f(sl)))(),
                         waits=[(sPE, sPE.count)], incs=[(sDV, 1)])
                    fmo_dv[(p, f)] = sDV.count
                fm1 = f - 1
                if fm1 >= 0:
                    sl1 = fm1 % 2
                    pk = 2 + sl1
                    waits = [(sDV, fmo_dv[(p, fm1)])]
                    pv = prev2(stt_dv, p, fm1, NFO)
                    if pv:
                        waits.append((sDV, pv))
                    if p == 0 and fm1 < 2:
                        waits.append((sDV, h_ready))
                    def fnT2(sl1=sl1, pk=pk):
                        def go(eng):
                            return [eng.transpose(bank_f(pk)[:, jj * 128:(jj + 1) * 128],
                                                  fmo[sl1][:, jj * 128:(jj + 1) * 128], idf[:])
                                    for jj in range(4)]
                        return go
                    P.op("tensor", fnT2(), waits=waits, incs=[(sPE, 1)])
                    waits = [(sPE, sPE.count), (sXSL[sl1], sXSL[sl1].count),
                             (sPC[sl1], sPC[sl1].count)]
                    def fnS(sl1=sl1, pk=pk):
                        def go(eng):
                            return eng.scalar_tensor_tensor(pcb[sl1][:], bank_f(pk), 0.5,
                                                            xsl[sl1][:], ALU.mult, ALU.add)
                        return go
                    P.op("vector", fnS(), waits=waits, incs=[(sDV, 1)])
                    stt_dv[(p, fm1)] = sDV.count
                    P.op("gpsimd",
                         (lambda sl1=sl1, fm1=fm1, row0=row0:
                          lambda eng: eng.dma_start(
                              bass.AP(dst.tensor, row0 * H + fm1 * 128,
                                      [[H, 128], [128 * H, 4], [1, 128]]),
                              pcb[sl1][:].rearrange("p (s m) -> p s m", s=4)))(),
                         waits=[(sDV, sDV.count)], incs=[(sPC[sl1], 16)])
                if p + 1 < NPASS and f in (6, 8, 10, 12):
                    emitA_s(p + 1, (f - 6) // 2)
        P.op("gpsimd", lambda eng: eng.nop(),
             waits=[(sPC[0], sPC[0].count), (sPC[1], sPC[1].count)])
        drains = [(s, s.count) for s in
                  [sID] + sXIN + sWG + sWU + sWD + sXSL]
        P.op("sync", lambda eng: eng.nop(), waits=drains)

    def emit_attn(P, st, seq, tag):
        x1g = st.enter_context(nc.sbuf_tensor(f"{tag}_x1g", [128, 4 * H], F32))
        xnm = [st.enter_context(nc.sbuf_tensor(f"{tag}_xnm{i}", [128, H], BF16)) for i in range(2)]
        xnT = st.enter_context(nc.sbuf_tensor(f"{tag}_xnT", [128, NI * QG], BF16))
        wqk = [st.enter_context(nc.sbuf_tensor(f"{tag}_wqk{i}", [128, NI * 256], BF16)) for i in range(2)]
        wrtb = st.enter_context(nc.sbuf_tensor(f"{tag}_wrt", [128, NI * 128], BF16))
        wvb = st.enter_context(nc.sbuf_tensor(f"{tag}_wv", [128, NI * 512], BF16))
        v4 = [st.enter_context(nc.sbuf_tensor(f"{tag}_v4{i}", [128, 4 * 512], BF16)) for i in range(2)]
        kwk = [st.enter_context(nc.sbuf_tensor(f"{tag}_kwk{i}", [128, S], BF16)) for i in range(2)]
        vwk = [st.enter_context(nc.sbuf_tensor(f"{tag}_vwk{i}", [128, S], BF16)) for i in range(2)]
        qwk = [st.enter_context(nc.sbuf_tensor(f"{tag}_qwk{i}", [128, QG], BF16)) for i in range(2)]
        rotbuf = st.enter_context(nc.sbuf_tensor(f"{tag}_rob", [64, 1024], F32))
        psb = [st.enter_context(nc.sbuf_tensor(f"{tag}_p{i}", [128, S], BF16)) for i in range(2)]
        ptb = st.enter_context(nc.sbuf_tensor(f"{tag}_pt", [128, 16 * 512], BF16))
        wob = ptb
        afm = st.enter_context(nc.sbuf_tensor(f"{tag}_afm", [128, NH * 512], BF16))
        tmp1 = st.enter_context(nc.sbuf_tensor(f"{tag}_tmp1", [64, 512], F32))
        tmp2 = st.enter_context(nc.sbuf_tensor(f"{tag}_tmp2", [64, 512], F32))
        otmp = st.enter_context(nc.sbuf_tensor(f"{tag}_otmp", [128, 512], F32))
        idb = st.enter_context(nc.sbuf_tensor(f"{tag}_idb", [128, 128], BF16))
        cosb = st.enter_context(nc.sbuf_tensor(f"{tag}_cos", [64, S], F32))
        sinb = st.enter_context(nc.sbuf_tensor(f"{tag}_sin", [64, S], F32))
        maskb = st.enter_context(nc.sbuf_tensor(f"{tag}_msk", [128, 128], F32))
        bqb = st.enter_context(nc.sbuf_tensor(f"{tag}_bq", [128, NH], F32))
        bkb = st.enter_context(nc.sbuf_tensor(f"{tag}_bk", [128, NH], F32))
        brqb = st.enter_context(nc.sbuf_tensor(f"{tag}_brq", [64, NH // 2], F32))
        brkb = st.enter_context(nc.sbuf_tensor(f"{tag}_brk", [64, NH // 2], F32))
        bvbb = st.enter_context(nc.sbuf_tensor(f"{tag}_bvb", [128, H], F32))
        bobb = st.enter_context(nc.sbuf_tensor(f"{tag}_bob", [128, H], F32))
        stt_ = st.enter_context(nc.sbuf_tensor(f"{tag}_st", [128, 32], F32))
        negmax = stt_[:, 28:29]
        den = st.enter_context(nc.sbuf_tensor(f"{tag}_den", [128, 2], F32))

        sCst = sem(f"{tag}cst")
        sX1 = [sem(f"{tag}x1_{i}") for i in range(4)]
        sWQK = [sem(f"{tag}wqk{i}") for i in range(2)]
        sWRT = sem(f"{tag}wrt")
        sWV = sem(f"{tag}wv")
        sWO = [sem(f"{tag}wo0")]
        sKH = [sem(f"{tag}kh{i}") for i in range(2)]
        sVH = [sem(f"{tag}vh{i}") for i in range(2)]
        sKW = sem(f"{tag}kw")
        sVW = sem(f"{tag}vw")
        sX2 = sem(f"{tag}x2")

        for dst_, src_ in ((idb, t_idb), (cosb, t_cos), (sinb, t_sin), (maskb, t_mask),
                           (bqb, t_bq), (bkb, t_bk), (brqb, t_brq), (brkb, t_brk),
                           (bvbb, t_bvb), (bobb, t_bob)):
            P.op("sync",
                 (lambda dst_=dst_, src_=src_: lambda eng: eng.dma_start(dst_[:], src_))(),
                 incs=[(sCst, 16)])
        cst_wm = sCst.count

        evacA, qk_pe, rot_pe, rot_evac, rope_dv, rope_ac_d = {}, {}, {}, {}, {}, {}
        vprojE, vev_d, vcur_d, kw_d = {}, {}, {}, {}
        scores_pe, exp_ac, tpos_pe, pscale_d = {}, {}, {}, {}
        pt_ac, pv_pe, attn_dv = {}, {}, {}
        op_pe, op_dv, xnm_dv = {}, {}, {}
        wqk_ld, wrt_ld, wv_ld, hist_kh, hist_vh = {}, {}, {}, {}, {}
        last_vev = [0]
        ptev_last = [0]

        for g in range(NG):
            grow = seq * S + g * QG
            pos0 = g * QG
            for s_ in range(4):
                sl = s_ % 2
                waits = [(sCst, cst_wm)]
                if g > 0 and s_ == 0:
                    waits.append((sX2, sX2.count))
                P.op("sync",
                     (lambda s_=s_, grow=grow:
                      lambda eng: eng.dma_start(x1g[:, s_ * H:(s_ + 1) * H],
                                                x1s[grow + s_ * 128:grow + (s_ + 1) * 128, :]))(),
                     waits=waits, incs=[(sX1[s_], 16)])
                w_in = [(sX1[s_], sX1[s_].count)]
                xg = x1g[:, s_ * H:(s_ + 1) * H]
                war = xnm_dv.get((g, s_ - 1)) or (xnm_dv.get((g - 1, 3)) if g else None)
                xnm_dv[(g, s_)] = emit_ln(P, stt_, xg, xnm[sl][:], w_in, war)
                pk = 4 + 2 * sl
                waits = [(sDV, sDV.count)]
                pv = evacA.get((g, s_ - 2)) or (evacA.get((g - 1, s_ + 2)) if g else None)
                if pv:
                    waits.append((sAC, pv))
                if g and s_ < 2:
                    waits.append((sDV, op_dv[(g - 1, 3, 3)]))
                if g and s_ == 0:
                    waits.append((sPE, qk_pe[(g - 1, NH - 1)]))
                def fnT(sl=sl, pk=pk):
                    def go(eng):
                        return [eng.transpose(bank_b(pk, 2)[:, I * 128:(I + 1) * 128],
                                              xnm[sl][:, I * 128:(I + 1) * 128], idb[:])
                                for I in range(NI)]
                    return go
                P.op("tensor", fnT(), waits=waits, incs=[(sPE, 1)])
                def fnE(pk=pk, s_=s_):
                    def go(eng):
                        dstp = bass.AP(xnT, s_ * 128, [[NI * QG, 128], [QG, NI], [1, 128]])
                        return eng.copy(dstp, bank_b(pk, 2))
                    return go
                P.op("scalar", fnE(), waits=[(sPE, sPE.count)], incs=[(sAC, 1)])
                evacA[(g, s_)] = sAC.count
            xnT_ready = sAC.count

            def ld_wqk(hh, g=g):
                slh = hh % 2
                waits = []
                pv = qk_pe.get((g, hh - 2)) or (qk_pe.get((g - 1, hh + NH - 2)) if g else None)
                if pv:
                    waits.append((sPE, pv))
                pvl = wqk_ld.get((g, hh - 2)) or (wqk_ld.get((g - 1, hh + NH - 2)) if g else None)
                if pvl:
                    waits.append((sWQK[slh], pvl))
                if ("qk", hh) in prep_ready:
                    waits.append((sems["prep"], prep_ready[("qk", hh)]))
                P.op("sync", (lambda hh=hh, slh=slh: lambda eng: eng.dma_start(wqk[slh][:], s_qk[hh]))(),
                     waits=waits, incs=[(sWQK[slh], 16)])
                wqk_ld[(g, hh)] = sWQK[slh].count

            def ld_hist(hh, g=g, pos0=pos0):
                if g == 0:
                    return
                par = hh % 2
                waits = [(sKW, sKW.count)]
                pv = scores_pe.get((g, hh - 2, 3)) or scores_pe.get((g - 1, hh + NH - 2, 3))
                if pv:
                    waits.append((sPE, pv))
                pvh = hist_kh.get((g, hh - 2)) or hist_kh.get((g - 1, hh + NH - 2))
                if pvh:
                    waits.append((sKH[par], pvh))
                P.op("sync",
                     (lambda hh=hh, g=g, par=par:
                      lambda eng: eng.dma_start(kwk[par][:, 0:g * QG],
                                                bass.AP(k_scr.tensor, (seq * NH + hh) * 128 * S,
                                                        [[S, 128], [1, g * QG]])))(),
                     waits=waits, incs=[(sKH[par], 16)])
                hist_kh[(g, hh)] = sKH[par].count
                waits = [(sVW, sVW.count)]
                pv = pv_pe.get((g, hh - 2)) or pv_pe.get((g - 1, hh + NH - 2))
                if pv:
                    waits.append((sPE, pv))
                pvh = hist_vh.get((g, hh - 2)) or hist_vh.get((g - 1, hh + NH - 2))
                if pvh:
                    waits.append((sVH[par], pvh))
                P.op("sync",
                     (lambda hh=hh, g=g, par=par:
                      lambda eng: eng.dma_start(vwk[par][:, 0:g * 512],
                                                bass.AP(v_scr.tensor,
                                                        (seq * NH + hh) * (S // 128) * 128 * 128,
                                                        [[128, 128], [128 * 128, g * 4], [1, 128]])))(),
                     waits=waits, incs=[(sVH[par], 16)])
                hist_vh[(g, hh)] = sVH[par].count

            def ld_wrt(j, g=g):
                waits = []
                pv = rot_pe.get((g, j - 1)) or (rot_pe.get((g - 1, j - 1 + NH // 2)) if g else None)
                if pv:
                    waits.append((sPE, pv))
                pvl = wrt_ld.get((g, j - 1)) or (wrt_ld.get((g - 1, NH // 2 - 1)) if g else None)
                if pvl:
                    waits.append((sWRT, pvl))
                if ("rot", j, 1, 1) in prep_ready:
                    waits.append((sems["prep"], prep_ready[("rot", j, 1, 1)]))
                P.op("sync", (lambda j=j: lambda eng: eng.dma_start(wrtb[:], s_rot[j]))(),
                     waits=waits, incs=[(sWRT, 16)])
                wrt_ld[(g, j)] = sWRT.count

            def ld_wv(s4, g=g):
                waits = []
                pv = vprojE.get((g, s4 - 1)) or (vprojE.get((g - 1, 3)) if g else None)
                if pv:
                    waits.append((sPE, pv))
                pvl = wv_ld.get((g, s4 - 1)) or (wv_ld.get((g - 1, 3)) if g else None)
                if pvl:
                    waits.append((sWV, pvl))
                if ("wv", s4, 3) in prep_ready:
                    waits.append((sems["prep"], prep_ready[("wv", s4, 3)]))
                P.op("sync", (lambda s4=s4: lambda eng: eng.dma_start(wvb[:], s_wv[s4]))(),
                     waits=waits, incs=[(sWV, 16)])
                wv_ld[(g, s4)] = sWV.count

            def emit_rot(j, g=g):
                for which in (0, 1):
                    waits = [(sWRT, wrt_ld[(g, j)]), (sAC, xnT_ready)]
                    if which == 0:
                        if last_vev[0]:
                            waits.append((sDV, last_vev[0]))
                        pvr = rot_evac.get((g, j - 1, 1)) or \
                            (rot_evac.get((g - 1, NH // 2 - 1, 1)) if g else None)
                        if pvr:
                            waits.append((sAC, pvr))
                    else:
                        waits.append((sAC, rot_evac[(g, j, 0)]))
                    def fnR(which=which):
                        def go(eng):
                            return [eng.matmul(bank_f(3)[0:64, :],
                                               wrtb[:, i * 128 + which * 64: i * 128 + (which + 1) * 64],
                                               xnT[:, i * QG:(i + 1) * QG],
                                               start=(i == 0), stop=(i == NI - 1))
                                    for i in range(NI)]
                        return go
                    P.op("tensor", fnR(), waits=waits, incs=[(sPE, 1)])
                    if which == 1:
                        rot_pe[(g, j)] = sPE.count
                    ew = [(sPE, sPE.count)]
                    pvd = rope_dv.get((g, 2 * j - 1)) or (rope_dv.get((g - 1, NH - 1)) if g else None)
                    if pvd:
                        ew.append((sDV, pvd))
                    P.op("scalar",
                         (lambda which=which: lambda eng: eng.copy(
                             rotbuf[:, which * 512:(which + 1) * 512], bank_f(3)[0:64, :]))(),
                         waits=ew, incs=[(sAC, 1)])
                    rot_evac[(g, j, which)] = sAC.count

            def emit_vproj(s4, g=g):
                sv4 = s4 % 2
                for s_ in range(4):
                    waits = [(sWV, wv_ld[(g, s4)]), (sAC, xnT_ready),
                             (sAC, rot_evac[(g, 2 * s4, 1)])]
                    if s_ > 0:
                        waits.append((sDV, sDV.count))
                    key = (g, s4 - 2) if s4 >= 2 else ((g - 1, s4 + 2) if g else None)
                    if key and (key[0], key[1] * 4 + 3) in vcur_d:
                        waits.append((sDV, vcur_d[(key[0], key[1] * 4 + 3)]))
                        waits.append((sVW, sVW.count))
                    def fnV(s_=s_):
                        def go(eng):
                            return [eng.matmul(bank_f(3),
                                               xnT[:, i * QG + s_ * 128: i * QG + (s_ + 1) * 128],
                                               wvb[:, i * 512:(i + 1) * 512],
                                               start=(i == 0), stop=(i == NI - 1))
                                    for i in range(NI)]
                        return go
                    P.op("tensor", fnV(), waits=waits, incs=[(sPE, 1)])
                    vprojE[(g, s4)] = sPE.count
                    P.op("vector",
                         (lambda sv4=sv4, s_=s_, s4=s4: lambda eng: eng.scalar_tensor_tensor(
                             v4[sv4][:, s_ * 512:(s_ + 1) * 512], bank_f(3), 0.0,
                             bvbb[:, s4 * 512:(s4 + 1) * 512], ALU.add, ALU.add))(),
                         waits=[(sPE, sPE.count)], incs=[(sDV, 1)])
                vev_d[(g, s4)] = sDV.count
                last_vev[0] = sDV.count
                for hh4 in range(4):
                    P.op("gpsimd",
                         (lambda s4=s4, hh4=hh4, sv4=sv4, g=g:
                          lambda eng: eng.dma_start(
                              bass.AP(v_scr.tensor,
                                      ((seq * NH + s4 * 4 + hh4) * (S // 128) + g * 4) * 128 * 128,
                                      [[128, 128], [128 * 128, 4], [1, 128]]),
                              bass.AP(v4[sv4], hh4 * 128, [[2048, 128], [512, 4], [1, 128]])))(),
                         waits=[(sDV, vev_d[(g, s4)])], incs=[(sVW, 16)])

            def emit_qk(hh, which, g=g):
                slh = hh % 2
                waits = [(sWQK[slh], wqk_ld[(g, hh)]), (sAC, xnT_ready)]
                pvd = rope_dv.get((g, hh - 1)) or (rope_dv.get((g - 1, NH - 1)) if g else None)
                pva = rope_ac_d.get((g, hh - 1)) or (rope_ac_d.get((g - 1, NH - 1)) if g else None)
                if pvd:
                    waits.append((sDV, pvd))
                if pva:
                    waits.append((sAC, pva))
                def fnQ(slh=slh, which=which):
                    def go(eng):
                        return [eng.matmul(bank_f(which),
                                           wqk[slh][:, i * 256 + which * 128: i * 256 + (which + 1) * 128],
                                           xnT[:, i * QG:(i + 1) * QG],
                                           start=(i == 0), stop=(i == NI - 1))
                                for i in range(NI)]
                    return go
                P.op("tensor", fnQ(), waits=waits, incs=[(sPE, 1)])
                if which == 1:
                    qk_pe[(g, hh)] = sPE.count

            def emit_rope(hh, g=g, pos0=pos0):
                e = hh % 2
                jp = hh // 2
                par = hh % 2
                lo, hi = e * 32, e * 32 + 32
                slot_war = scores_pe.get((g, hh - 2, 3)) or \
                    (scores_pe.get((g - 1, hh + NH - 2, 3)) if g else None)
                for which, (bt, brt, dst_ap) in enumerate((
                        (bqb, brqb, qwk[par][:]),
                        (bkb, brkb, kwk[par][:, pos0:pos0 + QG]))):
                    w0 = [(sPE, qk_pe[(g, hh)])]
                    if slot_war:
                        w0.append((sPE, slot_war))
                    if which == 1:
                        w0.append((sDV, sDV.count))
                        kwpv = kw_d.get((g, hh - 2)) or \
                            (kw_d.get((g - 1, hh + NH - 2)) if g else None)
                        if kwpv:
                            w0.append((sKW, kwpv))
                    P.op("vector",
                         (lambda which=which, bt=bt, hh=hh, lo=lo, hi=hi, pos0=pos0:
                          lambda eng: eng.scalar_tensor_tensor(
                              tmp1[lo:hi, :], bank_f(which)[lo:hi, :], bt[lo:hi, hh:hh + 1],
                              cosb[lo:hi, pos0:pos0 + QG], ALU.add, ALU.mult))(),
                         waits=w0)
                    P.op("vector",
                         (lambda which=which, brt=brt, jp=jp, lo=lo, hi=hi, pos0=pos0:
                          lambda eng: eng.scalar_tensor_tensor(
                              tmp2[lo:hi, :], rotbuf[lo:hi, which * 512:(which + 1) * 512],
                              brt[lo:hi, jp:jp + 1],
                              sinb[lo:hi, pos0:pos0 + QG], ALU.add, ALU.mult))(),
                         waits=[(sAC, rot_evac[(g, jp, which)])])
                    P.op("vector",
                         (lambda dst_ap=dst_ap, lo=lo, hi=hi:
                          lambda eng: eng.tensor_add(dst_ap[lo:hi, :], tmp1[lo:hi, :], tmp2[lo:hi, :]))(),
                         waits=[(sDV, sDV.count)])
                    if e == 0:
                        spans = ((32, 64), (64, 128))
                    else:
                        spans = ((0, 32), (64, 128))
                    for si, (p0, p1) in enumerate(spans):
                        P.op("scalar",
                             (lambda dst_ap=dst_ap, which=which, bt=bt, hh=hh, p0=p0, p1=p1:
                              lambda eng: eng.activation(
                                  dst_ap[p0:p1, :], bank_f(which)[p0:p1, :], AF.Identity,
                                  bias=bt[p0:p1, hh:hh + 1]))(),
                             waits=(w0 if si == 0 else ()),
                             incs=([(sAC, 1)] if si == 1 else ()))
                rope_dv[(g, hh)] = sDV.count
                rope_ac_d[(g, hh)] = sAC.count
                P.op("gpsimd",
                     (lambda hh=hh, pos0=pos0, par=par:
                      lambda eng: eng.dma_start(
                          bass.AP(k_scr.tensor, (seq * NH + hh) * 128 * S + pos0, [[S, 128], [1, QG]]),
                          kwk[par][:, pos0:pos0 + QG]))(),
                     waits=[(sDV, rope_dv[(g, hh)]), (sAC, rope_ac_d[(g, hh)]),
                            (sKW, sKW.count)], incs=[(sKW, 16)])
                kw_d[(g, hh)] = sKW.count

            def emit_vcur(hh, g=g):
                s4 = hh // 4
                sv4c = s4 % 2
                par = hh % 2
                waits = [(sDV, vev_d[(g, s4)])]
                pv = pv_pe.get((g, hh - 2)) or (pv_pe.get((g - 1, hh + NH - 2)) if g else None)
                if pv:
                    waits.append((sPE, pv))
                P.op("vector",
                     (lambda hh=hh, g=g, sv4c=sv4c, par=par: lambda eng: eng.tensor_copy(
                         bass.AP(vwk[par], g * 512, [[S, 128], [128, 4], [1, 128]]),
                         bass.AP(v4[sv4c], (hh % 4) * 128, [[2048, 128], [512, 4], [1, 128]])))(),
                     waits=waits, incs=[(sDV, 1)])
                vcur_d[(g, hh)] = sDV.count

            ld_wqk(0)
            ld_hist(0)
            ld_wrt(0)
            ld_wv(0)
            ld_wqk(1)
            ld_hist(1)
            emit_rot(0)
            emit_vproj(0)
            emit_qk(0, 0)
            emit_qk(0, 1)
            emit_rope(0)
            emit_vcur(0)

            for h in range(NH):
                par = h % 2

                def emit_scores(t, h=h, par=par, g=g):
                    nkb = 4 * g + t + 1
                    keys = nkb * 128
                    waits = [(sDV, rope_dv[(g, h)]), (sAC, rope_ac_d[(g, h)])]
                    if t == 0:
                        waits.append((sAC, xnT_ready))
                        if g > 0:
                            waits.append((sKH[par], hist_kh[(g, h)]))
                    pv = exp_ac.get((g, h, t - 1)) or (exp_ac.get((g, h - 1, 3)) if h else None)
                    if pv:
                        waits.append((sAC, pv))
                    def fnS2(keys=keys, t=t):
                        def go(eng):
                            return [eng.matmul(pp[:, 2048 + n0:2048 + min(n0 + 512, keys)],
                                               qwk[par][:, t * 128:(t + 1) * 128],
                                               kwk[par][:, n0:min(n0 + 512, keys)],
                                               start=True, stop=True)
                                    for n0 in range(0, keys, 512)]
                        return go
                    P.op("tensor", fnS2(), waits=waits, incs=[(sPE, 1)])
                    scores_pe[(g, h, t)] = sPE.count

                def emit_chain(t, h=h, par=par, g=g):
                    nkb = 4 * g + t + 1
                    keys = nkb * 128
                    P.op("vector",
                         (lambda nkb=nkb: lambda eng: eng.tensor_tensor(
                             pp[:, 2048 + (nkb - 1) * 128: 2048 + nkb * 128],
                             pp[:, 2048 + (nkb - 1) * 128: 2048 + nkb * 128],
                             maskb[:], ALU.add))(),
                         waits=[(sPE, scores_pe[(g, h, t)])])
                    P.op("vector",
                         (lambda keys=keys: lambda eng: eng.reduce_max(
                             negmax, pp[:, 2048:2048 + keys], axis=AX.X, negate=True))(),
                         waits=[(sDV, sDV.count)])
                    slp = t % 2
                    waits = [(sDV, sDV.count)]
                    pv = tpos_pe.get((g, h, t - 2)) or (tpos_pe.get((g, h - 1, t + 2)) if h else
                                                        (tpos_pe.get((g - 1, NH - 1, t + 2)) if g else None))
                    if pv:
                        waits.append((sPE, pv))
                    P.op("scalar",
                         (lambda slp=slp, keys=keys: lambda eng: eng.activation(
                             psb[slp][:, 0:keys], pp[:, 2048:2048 + keys], AF.Exp,
                             bias=negmax, accum_out=den[:, 0:1]))(),
                         waits=waits, incs=[(sAC, 1)])
                    exp_ac[(g, h, t)] = sAC.count
                    P.op("vector", lambda eng: eng.reciprocal(den[:, 1:2], den[:, 0:1]),
                         waits=[(sAC, sAC.count)])
                    P.op("vector",
                         (lambda slp=slp, keys=keys: lambda eng: eng.tensor_scalar_mul(
                             psb[slp][:, 0:keys], psb[slp][:, 0:keys], den[:, 1:2]))(),
                         waits=[(sDV, sDV.count)])
                    pscale_d[(g, h, t)] = sDV.count

                def emit_pt(t, h=h, g=g):
                    nkb = 4 * g + t + 1
                    slp = t % 2
                    for r0 in range(0, nkb, 8):
                        r1 = min(r0 + 8, nkb)
                        waits = [(sDV, pscale_d[(g, h, t)]), (sAC, ptev_last[0])]
                        if t == 0 and r0 == 0:
                            pv = attn_dv.get((g, h - 1)) or (attn_dv.get((g - 1, NH - 1)) if g else None)
                            if pv:
                                waits.append((sDV, pv))
                        def fnPT(slp=slp, r0=r0, r1=r1):
                            def go(eng):
                                return [eng.transpose(bank_b(2)[:, (kb - r0) * 128:(kb - r0 + 1) * 128],
                                                      psb[slp][:, kb * 128:(kb + 1) * 128], idb[:])
                                        for kb in range(r0, r1)]
                            return go
                        P.op("tensor", fnPT(), waits=waits, incs=[(sPE, 1)])
                        tpos_pe[(g, h, t)] = sPE.count
                        def fnPTE(r0=r0, r1=r1, t=t):
                            def go(eng):
                                dstp = bass.AP(ptb, r0 * 512 + t * 128,
                                               [[16 * 512, 128], [512, r1 - r0], [1, 128]])
                                return eng.copy(dstp, bank_b(2)[:, 0:(r1 - r0) * 128])
                            return go
                        P.op("scalar", fnPTE(), waits=[(sPE, sPE.count)], incs=[(sAC, 1)])
                        ptev_last[0] = sAC.count

                emit_scores(0)
                emit_chain(0)
                if h + 1 < NH:
                    emit_qk(h + 1, 0)
                emit_scores(1)
                emit_chain(1)
                emit_pt(0)
                if h + 1 < NH:
                    emit_qk(h + 1, 1)
                emit_scores(2)
                emit_chain(2)
                emit_pt(1)
                if h % 2 == 1 and h + 2 < NH:
                    emit_rot((h + 1) // 2)
                emit_scores(3)
                emit_chain(3)
                emit_pt(2)
                if h % 4 == 3 and h + 1 < NH:
                    emit_vproj(h // 4 + 1)
                if h + 1 < NH:
                    emit_rope(h + 1)
                    emit_vcur(h + 1)
                emit_pt(3)
                pt_ac[(g, h)] = ptev_last[0]

                waits = [(sAC, pt_ac[(g, h)]), (sDV, vcur_d[(g, h)])]
                if g > 0:
                    waits.append((sVH[par], hist_vh[(g, h)]))
                def fnPV(g=g, par=par):
                    def go(eng):
                        r = []
                        nkb = 4 * g + 4
                        for kb in range(nkb):
                            qs = max(0, (kb - 4 * g) * 128)
                            r.append(eng.matmul(bank_f(2)[:, qs:512],
                                                vwk[par][:, kb * 128:(kb + 1) * 128],
                                                ptb[:, kb * 512 + qs:(kb + 1) * 512],
                                                start=(kb == 0), stop=(kb == nkb - 1)))
                        return r
                    return go
                P.op("tensor", fnPV(), waits=waits, incs=[(sPE, 1)])
                pv_pe[(g, h)] = sPE.count
                P.op("vector",
                     (lambda h=h: lambda eng: eng.tensor_copy(afm[:, h * 512:(h + 1) * 512], bank_f(2)))(),
                     waits=[(sPE, pv_pe[(g, h)])], incs=[(sDV, 1)])
                attn_dv[(g, h)] = sDV.count

                if h + 2 < NH:
                    ld_wqk(h + 2)
                    ld_hist(h + 2)
                if h % 2 == 0 and h // 2 + 1 <= NH // 2 - 1 and h + 3 < NH:
                    ld_wrt(h // 2 + 1)
                if h % 4 == 1 and h // 4 + 1 <= 3:
                    ld_wv(h // 4 + 1)

            for gf in range(4):
                slo = 0
                waits = []
                pv = op_pe.get((g, gf - 1)) or (op_pe.get((g - 1, 3)) if g else None)
                if pv:
                    waits.append((sPE, pv))
                if gf == 0:
                    waits.append((sPE, pv_pe[(g, NH - 1)]))
                    waits.append((sAC, pt_ac[(g, NH - 1)]))
                if ("wo", gf) in prep_ready:
                    waits.append((sems["prep"], prep_ready[("wo", gf)]))
                P.op("sync", (lambda gf=gf, slo=slo: lambda eng: eng.dma_start(wob[:], s_wo[gf]))(),
                     waits=waits, incs=[(sWO[slo], 16)])
                for s_ in range(4):
                    bko = 4 + (gf * 4 + s_) % 4
                    waits = [(sWO[slo], sWO[slo].count), (sDV, attn_dv[(g, NH - 1)]),
                             (sAC, exp_ac[(g, NH - 1, 3)])]
                    pv = op_dv.get((g, gf - 1, s_)) or (op_dv.get((g - 1, 3, s_)) if g else None)
                    if pv:
                        waits.append((sDV, pv))
                    def fnO(s_=s_, bko=bko):
                        def go(eng):
                            return [eng.matmul(bank_f(bko),
                                               afm[:, hh * 512 + s_ * 128: hh * 512 + (s_ + 1) * 128],
                                               wob[:, hh * 512:(hh + 1) * 512],
                                               start=(hh == 0), stop=(hh == NH - 1))
                                    for hh in range(NH)]
                        return go
                    P.op("tensor", fnO(), waits=waits, incs=[(sPE, 1)])
                    op_pe[(g, gf)] = sPE.count
                    wot = [(sPE, sPE.count)]
                    pvo = op_dv.get((g, gf, s_ - 1)) or (op_dv.get((g, gf - 1, 3)) if gf else None)
                    if pvo:
                        wot.append((sDV, pvo))
                    P.op("vector",
                         (lambda bko=bko, gf=gf: lambda eng: eng.tensor_tensor(
                             otmp[:], bank_f(bko), bobb[:, gf * 512:(gf + 1) * 512], ALU.add))(),
                         waits=wot)
                    P.op("vector",
                         (lambda s_=s_, gf=gf: lambda eng: eng.tensor_add(
                             x1g[:, s_ * H + gf * 512: s_ * H + (gf + 1) * 512],
                             otmp[:],
                             x1g[:, s_ * H + gf * 512: s_ * H + (gf + 1) * 512]))(),
                         waits=[(sDV, sDV.count)])
                    op_dv[(g, gf, s_)] = sDV.count
            for s_ in range(4):
                waits = [(sDV, max(op_dv[(g, gf, s_)] for gf in range(4)))]
                P.op("gpsimd",
                     (lambda s_=s_, grow=grow: lambda eng: eng.dma_start(
                         x2s[grow + s_ * 128:grow + (s_ + 1) * 128, :], x1g[:, s_ * H:(s_ + 1) * H]))(),
                     waits=waits, incs=[(sX2, 16)])
        P.op("gpsimd", lambda eng: eng.nop(),
             waits=[(sX2, sX2.count), (sKW, sKW.count), (sVW, sVW.count)])
        drains = [(s, s.count) for s in
                  [sCst, sWRT, sWV] + sKH + sVH + sX1 + sWQK + sWO]
        P.op("sync", lambda eng: eng.nop(), waits=drains)


    if "f1" in phases:
        with contextlib.ExitStack() as ph:
            P1 = Prog(nc, auto=auto_map)
            phase_prologue(P1)
            sPR = emit_prep(P1) if "prep" in phases else None
            emit_ffn(P1, ph, x_in, x1s, "f1", sPR, prep_ready)
            P1.emit()
    for seq in range(SPC):
        if f"a{seq}" in phases:
            with contextlib.ExitStack() as ph:
                Pa = Prog(nc, auto=auto_map)
                phase_prologue(Pa)
                if seq == 0 and prep_ready.get("_batch2"):
                    sPR = sems["prep"]
                    for key, src_, dst_ in prep_ready.pop("_batch2"):
                        Pa.op("gpsimd",
                              (lambda src_=src_, dst_=dst_:
                               lambda eng: eng.dma_start(dst_, src_))(),
                              waits=[(sPR, sPR.count)], incs=[(sPR, 16)])
                        prep_ready[key] = sPR.count
                emit_attn(Pa, ph, seq, f"a{seq}")
                Pa.emit()
    if "f3" in phases:
        with contextlib.ExitStack() as ph:
            P3 = Prog(nc, auto=auto_map)
            phase_prologue(P3)
            emit_ffn(P3, ph, x2s, out, "f3", None, prep_ready)
            P3.emit()
    if "f3" not in phases:
        with (nc.sbuf_tensor("zzz", [128, 128], F32) as zz,
              nc.semaphore("zzs") as zzs, nc.semaphore("zzd") as zzd,
              nc.Block() as blk):
            @blk.vector
            def _(ve):
                ve.memset(zz[:], 0.0).then_inc(zzs, 1)
            @blk.gpsimd
            def _(gp):
                gp.wait_ge(zzs, 1)
                gp.dma_start(out[0:128, 0:128], zz[:]).then_inc(zzd, 16)
                gp.wait_ge(zzd, 16)
    st.close()
    return nc


def kernel(L0_attn_hidden_states, L0_attn_w_qkv, L0_attn_b_qkv, L0_attn_w_out,
           L0_attn_b_out, L0_ffn_w_gate, L0_ffn_w_up, L0_ffn_w_down):
    x = np.asarray(L0_attn_hidden_states, dtype=np.float32)
    wqkv = np.asarray(L0_attn_w_qkv, dtype=np.float32)
    bqkv = np.asarray(L0_attn_b_qkv, dtype=np.float32)
    wout = np.asarray(L0_attn_w_out, dtype=np.float32)
    bout = np.asarray(L0_attn_b_out, dtype=np.float32)

    wqkvp = _host_prep_wqkv(wqkv)
    wg_t = np.ascontiguousarray(np.asarray(L0_ffn_w_gate, np.float32).T)
    wu_t = np.ascontiguousarray(np.asarray(L0_ffn_w_up, np.float32).T)
    wd_t = np.ascontiguousarray(np.asarray(L0_ffn_w_down, np.float32).T)
    cosp, sinp, idbf, idf32, mask = _host_tables()

    bq3 = bqkv.reshape(NH, 3 * HD)
    op = np.array(_odd_perm())
    bq = np.zeros((128, NH), np.float32)
    bk = np.zeros((128, NH), np.float32)
    for h in range(NH):
        qb, kb_ = bq3[h, 0:128], bq3[h, 128:256]
        if h % 2 == 1:
            qb, kb_ = qb[op], kb_[op]
        bq[:, h], bk[:, h] = qb, kb_
    perm = np.array([(m + 16) % 32 for m in range(32)])
    brq = np.zeros((64, NH // 2), np.float32)
    brk = np.zeros((64, NH // 2), np.float32)
    for jp in range(NH // 2):
        for e in range(2):
            brq[e * 32:(e + 1) * 32, jp] = bq3[2 * jp + e, perm]
            brk[e * 32:(e + 1) * 32, jp] = bq3[2 * jp + e, 128 + perm]
    bv = bq3[:, 256:384].reshape(-1)
    bvb = np.ascontiguousarray(np.broadcast_to(bv, (128, H))).astype(np.float32)
    bob = np.ascontiguousarray(np.broadcast_to(bout, (128, H))).astype(np.float32)

    nc = build_program()
    shared = dict(wg_t=wg_t, wu_t=wu_t, wd_t=wd_t, wqkvp=wqkvp, wo=wout,
                  t_cos=cosp, t_sin=sinp, t_idb=idbf, t_idf=idf32, t_mask=mask,
                  t_bq=bq, t_bk=bk, t_brq=brq, t_brk=brk, t_bvb=bvb, t_bob=bob)
    in_maps = []
    for c in range(NCORE):
        m = dict(shared)
        m["x"] = np.ascontiguousarray(x[c * SPC:(c + 1) * SPC].reshape(TOK, H))
        in_maps.append(m)
    res = run_bass_kernel_spmd(nc, in_maps, core_ids=list(range(NCORE)))
    outs = [r["out"].reshape(SPC, S, H) for r in res.results]
    return np.concatenate(outs, axis=0)
